# version 18
# baseline (speedup 1.0000x reference)
import numpy as np
from contextlib import ExitStack

import concourse.bass as bass
import concourse.mybir as mybir
from concourse.bass_utils import run_bass_kernel_spmd

F32 = mybir.dt.float32
BF16 = mybir.dt.bfloat16
AX = mybir.AxisListType
ALU = mybir.AluOpType
ACT = mybir.ActivationFunctionType

NCORE = 8
SEQS = 2
T = 2048
C = 1024
NSUP = 4
NFC = 32

OFF_QKV = 0
OFF_LA = 24 * 1024
OFF_WP = OFF_LA + 256
OFF_V1 = OFF_WP + 8192
OFF_U2 = OFF_V1 + 8192
OFF_U1 = OFF_U2 + 8192
WCOLS = OFF_U1 + 2048

ENGS = ("sync", "gpsimd", "vector", "scalar", "tensor")


class Rec:

    def __init__(self):
        self.q = {e: [] for e in ENGS}
        self.cnt = {}
        self.own = {"vector": "v", "scalar": "s", "tensor": "t", "gpsimd": "g",
                    "sync": None}

    def sem(self, name):
        self.cnt[name] = 0

    def op(self, eng, f, waits=(), inc=None, amt=1):
        ws = tuple(w for w in waits if w is not None)
        tok = None
        if inc is not None:
            self.cnt[inc] += amt
            tok = (inc, self.cnt[inc])
        self.q[eng].append((ws, inc, amt, f))
        return tok

    def dma(self, eng, qname, out_ap, in_ap, waits=()):
        ws = tuple(w for w in waits if w is not None)
        self.cnt[qname] += 16
        tok = (qname, self.cnt[qname])

        def f(e, _o=out_ap, _i=in_ap):
            return e.dma_start(out=_o, in_=_i)

        self.q[eng].append((ws, qname, 16, f))
        return tok


def build_nc(dbg=None):
    nc = bass.Bass("TRN2", target_bir_lowering=False)

    x_h = nc.dram_tensor("x", [SEQS, T, C], F32, kind="ExternalInput").ap()
    wbig = nc.dram_tensor("wbig", [128, WCOLS], F32, kind="ExternalInput").ap()
    wsmall = nc.dram_tensor("wsmall", [128, 2304], F32, kind="ExternalInput").ap()
    wlb = nc.dram_tensor("wlb", [32, 4096], F32, kind="ExternalInput").ap()
    bqk_h = nc.dram_tensor("bqk", [128, 24], F32, kind="ExternalInput").ap()
    b1c_h = nc.dram_tensor("b1c", [128, 32], F32, kind="ExternalInput").ap()
    bp_h = nc.dram_tensor("bp", [128, 8], F32, kind="ExternalInput").ap()
    b2r_h = nc.dram_tensor("b2r", [128, 1024], F32, kind="ExternalInput").ap()
    out_h = nc.dram_tensor("out", [SEQS, T, C], F32, kind="ExternalOutput").ap()
    dbg_t = {}
    if dbg:
        for name, shape in (("d_ln1", [128, 4096]), ("d_q", [128, 4096]),
                            ("d_k", [128, 8 * T]), ("d_v", [128, 16 * 1040]),
                            ("d_plq", [32, 512]), ("d_y", [128, 4096]),
                            ("d_r8", [128, 8]), ("d_e0", [128, 1024]),
                            ("d_attn", [128, 4096]), ("d_x2", [128, 1024]),
                            ("d_ln3", [128, 4096]), ("d_P", [128, 1024]),
                            ("d_q2", [128, 1024]), ("d_yfm", [128, 4096])):
            if name in dbg.get("taps", ()):
                dbg_t[name] = nc.dram_tensor(name, shape, F32, kind="ExternalOutput").ap()
    wscr = nc.dram_tensor("wscr", [128, WCOLS], BF16, kind="Internal").ap()

    x_view = x_h.rearrange("s (n p) c -> s n p c", p=128)
    out_view = out_h.rearrange("s (n p) c -> s n p c", p=128)

    es = ExitStack()
    SB = lambda n, s, d: es.enter_context(nc.sbuf_tensor(n, s, d))
    PS = lambda n, s, d: es.enter_context(nc.psum_tensor(n, s, d))

    with es:
        iden = SB("iden", [128, 128], BF16)
        tri = SB("tri", [128, 128], BF16)
        cm20 = SB("cm20", [128, 1], F32)
        ceps = SB("ceps", [128, 1], F32)
        lbq_bf = SB("lbq_bf", [32, 3072], BF16)
        lbp_bf = SB("lbp_bf", [32, 1024], BF16)
        lap_bf = SB("lap_bf", [128, 256], BF16)
        v2_bf = SB("v2_bf", [128, 2048], BF16)
        bqk = SB("bqk_s", [128, 24], F32)
        b1c = SB("b1c_s", [128, 32], F32)
        bp = SB("bp_s", [128, 8], F32)
        b2r = SB("b2r_s", [128, 1024], F32)
        k_fm = SB("k_fm", [128, 8 * T], BF16)
        v_tm = SB("v_tm", [128, 16 * 1040], BF16)
        q_sup = SB("q_sup", [128, 4096], BF16)
        e_bf = [SB(f"e_bf{i}", [128, 1024], BF16) for i in range(4)]
        ln1_fm = SB("ln1_fm", [128, 4096], BF16)
        ln1_tm = SB("ln1_tm", [128, 1024], BF16)
        ln3_tm = SB("ln3_tm", [128, 1024], BF16)
        ln3_fm = SB("ln3_fm", [128, 4096], BF16)
        y_fm2 = [SB(f"y_fm{i}", [128, 4096], BF16) for i in range(2)]
        rb = SB("rb", [33, 512], BF16)
        ones_sel = SB("ones_sel", [33, 128], BF16)
        attn_tm = SB("attn_tm", [128, 4096], BF16)
        vf_bf = [SB(f"vf_bf{i}", [128, 512], BF16) for i in range(2)]
        po_bf = [SB(f"po_bf{i}", [128, 512], BF16) for i in range(2)]
        P_bf = SB("P_bf", [128, 1024], BF16)
        h_bf = [SB(f"h_bf{i}", [128, 512], BF16) for i in range(2)]
        q2_bf = SB("q2_bf", [128, 1024], BF16)
        plq_bf = SB("plq_bf", [32, 512], BF16)
        plp_bf = SB("plp_bf", [32, 512], BF16)
        ot_bf = SB("ot_bf", [128, 1024], BF16)
        scr_bf = SB("scr_bf", [128, 1024], BF16)
        o1 = SB("o1", [128, 1024], F32)
        stat = SB("stat", [128, 128], F32)
        xa = [SB(f"xa{i}", [128, 1024], F32) for i in range(2)]
        x2t = [SB(f"x2t{i}", [128, 1024], F32) for i in range(4)]
        wq_st = [SB(f"wq_st{i}", [128, 1024], BF16) for i in range(2)]
        wp_st = [SB(f"wp_st{i}", [128, 1024], BF16) for i in range(2)]
        vu_st = [SB(f"vu_st{i}", [128, 256], BF16) for i in range(2)]
        u2_st = [SB(f"u2_st{i}", [128, 256], BF16) for i in range(2)]

        pA = PS("pA", [128, 512], F32)
        pB = PS("pB", [128, 512], F32)
        pC = PS("pC", [128, 512], F32)
        pAtt = PS("pAtt", [128, 1024], F32)
        pY0 = PS("pY0", [128, 512], F32)
        pY1 = PS("pY1", [128, 512], F32)
        pT = PS("pT", [128, 1024], BF16)
        pool = [pA, pB, pC]

        rec = Rec()
        for s_ in ("qw", "qx", "qg", "g", "v", "s", "t"):
            rec.sem(s_)

        st_ = {
            "pt_free": None, "pool_free": [None, None, None],
            "xa_free": [None, None], "wq_free": [None, None],
            "wp_free": [None, None], "yfm_free": [None, None],
            "vu_free": [None, None], "u2_free": [None, None],
            "vf_free": [None, None], "po_free": [None, None],
            "h_free": [None, None], "x2_free": [None] * 4,
            "ebf_free": [None] * 4, "pY_free": None,
            "attexp_last": None, "ln1_tm_free": None, "ln3_tm_free": None,
            "ln1_fm_free": None, "ln3_fm_free": None,
            "attn_tm_free": None, "qsup_free": None,
            "kv_free": None, "ot_free": None,
            "o1_free": None, "q2_free": None,
        }

        def gp_consts(g):
            g.memset(iden[:], 1.0)
            g.affine_select(out=iden[:], in_=iden[:], compare_op=ALU.is_equal,
                            fill=0.0, base=0, pattern=[[-1, 128]],
                            channel_multiplier=1)
            g.memset(tri[:], 0.0)
            g.affine_select(out=tri[:], in_=tri[:], compare_op=ALU.is_gt,
                            fill=1.0, base=0, pattern=[[-1, 128]],
                            channel_multiplier=1)
            g.memset(cm20[:], -20.0)
            g.memset(ceps[:], 1e-5)
            return g.memset(
                v_tm[:].rearrange("p (k h d) -> p k h d", h=16, d=65)[:, :, :, 64:65],
                1.0)

        t_consts = rec.op("gpsimd", gp_consts, inc="g")

        def v_consts(v):
            v.memset(rb[:], 1.0)
            v.memset(ones_sel[:], 0.0)
            v.memset(ones_sel[0:1, 0:64], 1.0)
            return v.memset(ones_sel[32:33, 64:128], 1.0)

        t_vconsts = rec.op("vector", v_consts, inc="v")

        t_bias = None
        for dst, src in ((bqk, bqk_h), (b1c, b1c_h), (bp, bp_h), (b2r, b2r_h)):
            t_bias = rec.dma("sync", "qw", dst[:], src)

        wtok = {}
        cv_state = {"cv_tok": None}

        def wtok_for(off, width):
            for (o, w, key) in CONV_CHUNKS:
                if o <= off and off + width <= o + w:
                    assert key in wtok, (off, width, key)
                    return wtok[key]
            raise AssertionError((off, width))

        CONV_CHUNKS = (
            [(OFF_LA, 256, "la")]
            + [(m * 1024, 1024, f"qkv{m}") for m in range(16, 24)]
            + [(m * 1024, 1024, f"qkv{m}") for m in range(8, 16)]
            + [(m * 1024, 1024, f"qkv{m}") for m in range(8)]
            + [(OFF_WP + m * 1024, 1024, f"wp{m}") for m in range(8)]
            + [(OFF_U1 + m * 1024, 1024, f"u1{m}") for m in range(2)]
            + [x for c in range(8) for x in
               ((OFF_V1 + c * 1024, 1024, f"v1{c}"),
                (OFF_U2 + c * 1024, 1024, f"u2{c}"))]
        )

        def gen_conv():
            smalls = [
                (v2_bf[:, 0:1024], wsmall[:, 0:1024], 128),
                (v2_bf[:, 1024:2048], wsmall[:, 1024:2048], 128),
                (lap_bf[:, 0:256], wsmall[:, 2048:2304], 128),
                (lbq_bf[:, 0:1024], wlb[0:32, 0:1024], 32),
                (lbq_bf[:, 1024:2048], wlb[0:32, 1024:2048], 32),
                (lbq_bf[:, 2048:3072], wlb[0:32, 2048:3072], 32),
                (lbp_bf[:, 0:1024], wlb[0:32, 3072:4096], 32),
            ]
            fst = [x2t[0], x2t[1]]
            fguard = [None, None]
            for i, (dst, src, np_) in enumerate(smalls):
                par = i % 2
                nf = src.shape[-1]
                t_ld = rec.dma("sync", "qw", fst[par][0:np_, 0:nf], src,
                               waits=[fguard[par]])
                tok = rec.op(
                    "vector",
                    lambda v, d=dst, p=par, n0=np_, n1=nf: v.tensor_copy(
                        d, fst[p][0:n0, 0:n1]),
                    waits=[t_ld], inc="v")
                fguard[par] = tok
                cv_state["cv_tok"] = tok
                if i % 2 == 1:
                    yield
            fbig = [o1, x2t[2]]
            bbig = [q2_bf, ot_bf]
            fg = [None, None]
            bg = [None, None]
            for i, (off, w, key) in enumerate(CONV_CHUNKS):
                par = i % 2
                t_ld = rec.dma("sync", "qw", fbig[par][:, 0:w],
                               wbig[:, off:off + w], waits=[fg[par]])
                t_cv = rec.op(
                    "vector",
                    lambda v, p=par, w_=w: v.tensor_copy(bbig[p][:, 0:w_],
                                                         fbig[p][:, 0:w_]),
                    waits=[t_ld, bg[par]], inc="v")
                fg[par] = t_cv
                t_st = rec.dma("sync", "qw", wscr[:, off:off + w],
                               bbig[par][:, 0:w], waits=[t_cv])
                bg[par] = t_st
                wtok[key] = t_st
                if i % 2 == 1:
                    yield
            st_["x2_free"] = [fguard[0], fguard[1], fg[1], None]
            st_["o1_free"] = fg[0]
            st_["q2_free"] = bg[0]
            st_["ot_free"] = bg[1]

        def low_prec(fn, *a, **kw):
            with nc.allow_low_precision(reason="softmax reciprocal in bf16"):
                return fn(*a, **kw)

        def mm(te, out_ap, lh, rh, start, stop, tp=None):
            return te.matmul(out_ap, lh, rh, start=start, stop=stop,
                             tile_position=tp)

        def transpose_group(src_fn, nblk, waits):
            def f(te):
                last = None
                for c in range(nblk):
                    last = te.transpose(pT[:, c * 128:(c + 1) * 128],
                                        src_fn(c), iden[:])
                return last
            return rec.op("tensor", f,
                          waits=list(waits) + [st_["pt_free"], t_consts],
                          inc="t")

        def emit_stats(t_sum, t_sq, base, nb=4):
            c = lambda j: stat[:, base + j * nb: base + (j + 1) * nb]
            t1 = rec.op("scalar", lambda s2: s2.activation(
                c(2), c(0), ACT.Identity, scale=1.0 / 1024), waits=[t_sum], inc="s")
            t2 = rec.op("vector", lambda v: v.scalar_tensor_tensor(
                out=c(3), in0=c(2), scalar=-1.0, in1=c(2),
                op0=ALU.mult, op1=ALU.mult), waits=[t1], inc="v")
            t2b = rec.op("vector", lambda v: v.tensor_scalar(
                out=c(6), in0=c(1), scalar1=1.0 / 1024, scalar2=ceps[:],
                op0=ALU.mult, op1=ALU.add), waits=[t_sq, t_consts], inc="v")
            t2c = rec.op("vector", lambda v: v.tensor_add(c(6), c(6), c(3)),
                         waits=[t2, t2b], inc="v")
            t3 = rec.op("scalar", lambda s2: s2.activation(
                c(7), c(6), ACT.Ln), waits=[t2c], inc="s")
            t4 = rec.op("scalar", lambda s2: s2.activation(
                c(4), c(7), ACT.Exp, scale=-0.5), waits=[t3], inc="s")
            t5 = rec.op("vector", lambda v: v.scalar_tensor_tensor(
                out=c(5), in0=c(2), scalar=-1.0, in1=c(4),
                op0=ALU.mult, op1=ALU.mult), waits=[t4], inc="v")
            return c(4), c(5), t4, t5

        def tapd(name, buf_ap, tok):
            if not (dbg and name in dbg_t):
                return
            w = buf_ap.shape[-1]
            npart = buf_ap.shape[0]
            done = 0
            while done < w:
                seg = min(1024, w - done)
                t_cv = rec.op("vector",
                              lambda v, b=buf_ap, d0=done, sg=seg, np_=npart:
                              v.tensor_copy(o1[0:np_, 0:sg], b[0:np_, d0:d0 + sg]),
                              waits=[tok, st_.get("dbg_free")], inc="v")
                st_["dbg_free"] = rec.dma("scalar", "qx",
                                          dbg_t[name][0:npart, done:done + seg],
                                          o1[0:npart, 0:seg], waits=[t_cv])
                done += seg

        seqd = {sq: {"k": [], "v": []} for sq in range(SEQS)}
        sup_tok = {}
        done_att = {}

        def gen_p2(seq, s):
            g = seq * NSUP + s
            ln1_ev = None
            for tl in range(4):
                tch = s * 4 + tl
                par = tl % 2
                base = par * 8
                t_x = rec.dma("gpsimd", "qx", xa[par][:],
                              x_view[seq, tch, :, :],
                              waits=[st_["xa_free"][par]])
                t_sum = rec.op("vector", lambda v, p=par, b=base: v.reduce_sum(
                    stat[:, b:b + 1], xa[p][:], axis=AX.X),
                    waits=[t_x], inc="v")
                t_sq = rec.op("vector", lambda v, p=par, b=base: v.scalar_tensor_tensor(
                    out=scr_bf[:], in0=xa[p][:], scalar=1.0, in1=xa[p][:],
                    op0=ALU.mult, op1=ALU.mult,
                    accum_out=stat[:, b + 1:b + 2]),
                    waits=[t_sum], inc="v")
                r1, mr1, tr, tmr = emit_stats(t_sum, t_sq, base, nb=1)
                t_n = rec.op("scalar", lambda s2, p=par, r_=r1, m_=mr1:
                             s2.activation(ln1_tm[:], xa[p][:], ACT.Identity,
                                           bias=m_, scale=r_),
                             waits=[tr, tmr, st_["ln1_tm_free"]], inc="s")
                st_["xa_free"][par] = t_n
                t_tp = transpose_group(
                    lambda c: ln1_tm[:, c * 128:(c + 1) * 128], 8,
                    [t_n, st_["ln1_fm_free"]])
                st_["ln1_tm_free"] = t_tp
                ln1_ev = rec.op("vector", lambda v, tl_=tl: v.tensor_copy(
                    ln1_fm[:].rearrange("p (c x) -> p c x", x=512)
                    [:, :, tl_ * 128:(tl_ + 1) * 128],
                    pT[:].rearrange("p (c x) -> p c x", x=128)),
                    waits=[t_tp], inc="v")
                st_["pt_free"] = ln1_ev
                yield
            ln1_ready = ln1_ev
            if dbg and g == 0:
                tapd("d_ln1", ln1_fm[:], ln1_ready)

            t_la = rec.dma("sync", "qw", wq_st[0][:, 0:256],
                           wscr[:, OFF_LA:OFF_LA + 256],
                           waits=[st_["wq_free"][0], wtok_for(OFF_LA, 256)])

            def f_la(te):
                last = None
                for c in range(8):
                    last = mm(te, pC[0:32, :],
                              wq_st[0][:, c * 32:(c + 1) * 32],
                              ln1_fm[:, c * 512:(c + 1) * 512],
                              c == 0, c == 7)
                return last
            t_lam = rec.op("tensor", f_la,
                           waits=[t_la, ln1_ready, st_["pool_free"][2],
                                  cv_state["cv_tok"]], inc="t")
            t_plq = rec.op("scalar", lambda s2: s2.activation(
                plq_bf[:, :], pC[0:32, :], ACT.Identity, scale=0.5),
                waits=[t_lam], inc="s")
            st_["pool_free"][2] = t_plq
            st_["wq_free"][0] = t_lam
            yield

            q_ready = None
            t_mm = None
            morder = list(range(16, 24)) + list(range(8, 16)) + list(range(8))
            for mi, m in enumerate(morder):
                if m == 0 and g > 0:
                    while not done_att.get(g - 1, False):
                        yield
                par = mi % 2
                bi = mi % 2
                bank = pool[bi]
                t_w = rec.dma("sync", "qw", wq_st[par][:],
                              wscr[:, OFF_QKV + m * 1024:
                                   OFF_QKV + (m + 1) * 1024],
                              waits=[st_["wq_free"][par],
                                     wtok_for(OFF_QKV + m * 1024, 1024)])

                def f_m(te, p=par, bk=bank, m_=m):
                    last = None
                    for c in range(8):
                        last = mm(te, bk[:],
                                  wq_st[p][:, c * 128:(c + 1) * 128],
                                  ln1_fm[:, c * 512:(c + 1) * 512],
                                  c == 0, False)
                    return mm(te, bk[:],
                              lbq_bf[0:32, m_ * 128:(m_ + 1) * 128],
                              plq_bf[0:32, :], False, True)
                t_mm = rec.op("tensor", f_m,
                              waits=[t_w, ln1_ready, t_plq,
                                     st_["pool_free"][bi], cv_state["cv_tok"]],
                              inc="t")
                st_["wq_free"][par] = t_mm
                if m < 8:
                    t_ev = rec.op("scalar",
                                  lambda s2, bk=bank, m_=m: s2.activation(
                                      q_sup[:, m_ * 512:(m_ + 1) * 512],
                                      bk[:], ACT.Identity,
                                      bias=bqk[:, m_:m_ + 1]),
                                  waits=[t_mm, t_bias, st_["qsup_free"]],
                                  inc="s")
                    st_["pool_free"][bi] = t_ev
                    if m == 7:
                        q_ready = t_ev
                elif m < 16:
                    mk = m - 8
                    t_ev = rec.op("scalar",
                                  lambda s2, bk=bank, mk_=mk, m_=m, s_=s:
                                  s2.activation(
                                      k_fm[:, mk_ * 2048 + s_ * 512:
                                           mk_ * 2048 + (s_ + 1) * 512],
                                      bk[:], ACT.Identity,
                                      bias=bqk[:, m_:m_ + 1]),
                                  waits=[t_mm, t_bias, st_["kv_free"]],
                                  inc="s")
                    st_["pool_free"][bi] = t_ev
                    if m == 15:
                        seqd[seq]["k"].append(t_ev)
                else:
                    mv = m - 16
                    vp = mi % 2
                    t_ev = rec.op("scalar",
                                  lambda s2, bk=bank, m_=m, vp_=vp:
                                  s2.activation(vf_bf[vp_][:], bk[:],
                                                ACT.Identity,
                                                bias=bqk[:, m_:m_ + 1]),
                                  waits=[t_mm, t_bias,
                                         st_["vf_free"][vp]], inc="s")
                    st_["pool_free"][bi] = t_ev
                    t_tp = transpose_group(
                        lambda c, vp_=vp: vf_bf[vp_][:, c * 128:(c + 1) * 128],
                        4, [t_ev, st_["kv_free"]])
                    st_["vf_free"][vp] = t_tp
                    t_vev = rec.op("vector", lambda v, mv_=mv, s_=s: v.tensor_copy(
                        v_tm[:].rearrange("p (k h d) -> p k h d", h=16, d=65)
                        [:, s_ * 4:(s_ + 1) * 4, 2 * mv_:2 * mv_ + 2, 0:64],
                        pT[:, 0:512].rearrange("p (k hd) -> p k hd", hd=128)
                        .rearrange("p k (h d) -> p k h d", d=64)),
                        waits=[t_tp], inc="v")
                    st_["pt_free"] = t_vev
                    if m == 23:
                        seqd[seq]["v"].append(t_vev)
                yield
            st_["ln1_fm_free"] = t_mm
            sup_tok[g] = {"q": q_ready}
            if dbg and g == 0:
                tapd("d_q", q_sup[:], q_ready)
                tapd("d_k", k_fm[:], q_ready)
                tapd("d_v", v_tm[:], q_ready)
                tapd("d_plq", plq_bf[:], t_plq)

        def gen_att(seq, s):
            g = seq * NSUP + s
            y_fm = y_fm2[g % 2]
            q_ready = sup_tok[g]["q"]
            k_list = seqd[seq]["k"]
            v_list = seqd[seq]["v"]
            pv_banks = (pY0, pY1)
            mul_last = None
            att_pv_last = None
            for pair in range(8):
                pv_stop = [None, None]
                for kc in range(s + 1):
                    diag = kc == s
                    exp_toks = [None] * 4
                    memset_toks = [None] * 4
                    for kk in range(4):
                        ki = kc * 4 + kk
                        q0 = 0

                        def f_qk(te, pr=pair, ki_=ki):
                            mm(te, pAtt[:, 0:512],
                               k_fm[0:64, pr * 2048 + ki_ * 128:
                                    pr * 2048 + (ki_ + 1) * 128],
                               q_sup[0:64, pr * 512:(pr + 1) * 512],
                               True, True, tp=(0, 0))
                            return mm(te, pAtt[:, 512:1024],
                                      k_fm[64:128, pr * 2048 + ki_ * 128:
                                           pr * 2048 + (ki_ + 1) * 128],
                                      q_sup[64:128, pr * 512:(pr + 1) * 512],
                                      True, True, tp=(64, 0))
                        t_qk = rec.op("tensor", f_qk,
                                      waits=[st_["attexp_last"], q_ready,
                                             k_list[kc]], inc="t")
                        t_exp = rec.op("scalar", lambda s2, kk_=kk, q0_=q0:
                                       s2.activation(
                                           e_bf[kk_][:].rearrange(
                                               "p (h x) -> p h x", x=512)[:, :, q0_:512],
                                           pAtt[:].rearrange(
                                               "p (h x) -> p h x", x=512)[:, :, q0_:512],
                                           ACT.Exp, scale=0.125,
                                           bias=cm20[:]),
                                       waits=[t_qk], inc="s")
                        st_["attexp_last"] = t_exp
                        exp_toks[kk] = t_exp
                        yield
                    mask_tok = None
                    if diag:
                        def f_mask(v):
                            last = None
                            for kk in range(4):
                                if kk > 0:
                                    last = v.memset(
                                        e_bf[kk][:].rearrange(
                                            "p (h x) -> p h x", x=512)
                                        [:, :, 0:kk * 128], 0.0)
                                sl_ap = (e_bf[kk][:]
                                         .rearrange("p (h x) -> p h x", x=512)
                                         [:, :, kk * 128:(kk + 1) * 128])
                                last = v.tensor_tensor(
                                    out=sl_ap, in0=sl_ap,
                                    in1=tri[:].unsqueeze(1)
                                    .broadcast_to([128, 2, 128]),
                                    op=ALU.mult)
                            return last
                        mask_tok = rec.op("vector", f_mask,
                                          waits=[exp_toks[3], t_consts],
                                          inc="v")

                    for kk in range(4):
                        ki = kc * 4 + kk

                        def f_pv(te, kc_=kc, pr=pair, s_=s, ki_=ki, kk_=kk):
                            last = None
                            for hh in range(2):
                                last = mm(
                                    te, pv_banks[hh][0:65, :],
                                    v_tm[:, ki_ * 1040 + (2 * pr + hh) * 65:
                                         ki_ * 1040 + (2 * pr + hh) * 65 + 65],
                                    e_bf[kk_][:, hh * 512:(hh + 1) * 512],
                                    kc_ == 0 and kk_ == 0,
                                    kc_ == s_ and kk_ == 3)
                            return last
                        w_pv = [mask_tok if diag else exp_toks[kk],
                                memset_toks[kk], v_list[kc], t_consts]
                        if kc == 0 and kk == 0:
                            w_pv += [st_["pY_free"]]
                        t_pv = rec.op("tensor", f_pv, waits=w_pv, inc="t")
                        if kc == s and kk == 3:
                            pv_stop = [t_pv, t_pv]
                        if kk % 2 == 1:
                            yield
                    att_pv_last = t_pv

                t_cpy = [None, None]
                t_rcp = [None, None]
                for hh in range(2):
                    bank = pv_banks[hh]
                    t_cpy[hh] = rec.op(
                        "vector",
                        lambda v, b=bank, hh_=hh, pr=pair: v.tensor_copy(
                            y_fm[hh_ * 64:hh_ * 64 + 64,
                                 pr * 512:(pr + 1) * 512],
                            b[0:64, :]),
                        waits=[pv_stop[hh], st_["yfm_free"][g % 2]],
                        inc="v")
                    t_rcp[hh] = rec.op(
                        "vector",
                        lambda v, b=bank, hh_=hh: low_prec(
                            v.reciprocal, rb[hh_ * 32:hh_ * 32 + 1, :],
                            b[64:65, :]),
                        waits=[pv_stop[hh]], inc="v")
                for hh in range(2):
                    ob = pv_banks[1 - hh]
                    t_bc = rec.op(
                        "tensor",
                        lambda te, ob_=ob, hh_=hh: te.matmul(
                            ob_[0:64, :],
                            ones_sel[0:33, hh_ * 64:hh_ * 64 + 64],
                            rb[0:33, :], start=True, stop=True),
                        waits=[t_rcp[hh], t_cpy[1 - hh], t_vconsts],
                        inc="t")
                    t_mul = rec.op(
                        "vector",
                        lambda v, ob_=ob, hh_=hh, pr=pair: v.tensor_tensor(
                            out=y_fm[hh_ * 64:hh_ * 64 + 64,
                                     pr * 512:(pr + 1) * 512],
                            in0=y_fm[hh_ * 64:hh_ * 64 + 64,
                                     pr * 512:(pr + 1) * 512],
                            in1=ob_[0:64, :], op=ALU.mult),
                        waits=[t_bc, t_cpy[hh]], inc="v")
                    mul_last = t_mul
                st_["pY_free"] = mul_last
                yield
            st_["qsup_free"] = att_pv_last
            sup_tok[g]["yfm"] = mul_last
            if s == NSUP - 1:
                st_["kv_free"] = att_pv_last

        def gen_f(seq, s):
            g = seq * NSUP + s
            y_fm = y_fm2[g % 2]
            yfm_ev = sup_tok[g]["yfm"]

            def f_lp(te):
                last = None
                for c in range(8):
                    last = mm(te, pC[0:32, :],
                              lap_bf[:, c * 32:(c + 1) * 32],
                              y_fm[:, c * 512:(c + 1) * 512], c == 0, c == 7)
                return last
            t_lpm = rec.op("tensor", f_lp,
                           waits=[yfm_ev, st_["pool_free"][2], cv_state["cv_tok"]],
                           inc="t")
            t_plp = rec.op("scalar", lambda s2: s2.activation(
                plp_bf[:, :], pC[0:32, :], ACT.Identity, scale=0.5),
                waits=[t_lpm], inc="s")
            st_["pool_free"][2] = t_plp
            yield

            last_proj_mm = None
            attn_ev = None
            for m in range(8):
                par = m % 2
                bi = m % 2
                bank = pool[bi]
                t_w = rec.dma("sync", "qw", wp_st[par][:],
                              wscr[:, OFF_WP + m * 1024:
                                   OFF_WP + (m + 1) * 1024],
                              waits=[st_["wp_free"][par],
                                     wtok_for(OFF_WP + m * 1024, 1024)])

                def f_pm(te, p=par, bk=bank, m_=m):
                    last = None
                    for c in range(8):
                        last = mm(te, bk[:],
                                  wp_st[p][:, c * 128:(c + 1) * 128],
                                  y_fm[:, c * 512:(c + 1) * 512],
                                  c == 0, False)
                    return mm(te, bk[:],
                              lbp_bf[0:32, m_ * 128:(m_ + 1) * 128],
                              plp_bf[0:32, :], False, True)
                t_mm2 = rec.op("tensor", f_pm,
                               waits=[t_w, yfm_ev, t_plp, cv_state["cv_tok"],
                                      st_["pool_free"][bi]], inc="t")
                st_["wp_free"][par] = t_mm2
                last_proj_mm = t_mm2
                t_ev = rec.op("scalar",
                              lambda s2, bk=bank, m_=m, p=par: s2.activation(
                                  po_bf[p][:], bk[:], ACT.Identity,
                                  bias=bp[:, m_:m_ + 1]),
                              waits=[t_mm2, t_bias, st_["po_free"][par]],
                              inc="s")
                st_["pool_free"][bi] = t_ev
                t_tp = transpose_group(
                    lambda c, p=par: po_bf[p][:, c * 128:(c + 1) * 128], 4,
                    [t_ev])
                st_["po_free"][par] = t_tp
                attn_ev = rec.op("vector", lambda v, m_=m: v.tensor_copy(
                    attn_tm[:].rearrange("p (q x) -> p q x", x=1024)
                    [:, :, m_ * 128:(m_ + 1) * 128],
                    pT[:, 0:512].rearrange("p (q x) -> p q x", x=128)),
                    waits=[t_tp, st_["attn_tm_free"]], inc="v")
                st_["pt_free"] = attn_ev
                yield
            st_["yfm_free"][g % 2] = last_proj_mm
            attn_ready = attn_ev
            if dbg and g == 0:
                tapd("d_attn", attn_tm[:], attn_ev)

            ln3_ev = None
            B2, B3 = 32, 64
            sq2_tok = None
            x_toks = []
            for qi in range(4):
                tch = s * 4 + qi
                a_sl = attn_tm[:, qi * 1024:(qi + 1) * 1024]
                t_x = rec.dma("gpsimd", "qx", x2t[qi][:],
                              x_view[seq, tch, :, :],
                              waits=[st_["x2_free"][qi]])
                x_toks.append(t_x)
                t_sum = rec.op("vector", lambda v, qi_=qi, a=a_sl: v.reduce_sum(
                    stat[:, B2 + qi_:B2 + qi_ + 1], a, axis=AX.X),
                    waits=[attn_ready], inc="v")
                sq2_tok = rec.op("vector", lambda v, qi_=qi, a=a_sl:
                                 v.scalar_tensor_tensor(
                                     out=scr_bf[:], in0=a, scalar=1.0, in1=a,
                                     op0=ALU.mult, op1=ALU.mult,
                                     accum_out=stat[:, B2 + 4 + qi_:B2 + 5 + qi_]),
                                 waits=[t_sum], inc="v")
            yield
            r2, mr2, tr2, tmr2 = emit_stats(sq2_tok, sq2_tok, B2)
            sq3_tok = None
            for qi in range(4):
                a_sl = attn_tm[:, qi * 1024:(qi + 1) * 1024]
                t_n2 = rec.op("scalar", lambda s2, qi_=qi, a=a_sl:
                              s2.activation(ot_bf[:], a, ACT.Identity,
                                            bias=mr2[:, qi_:qi_ + 1],
                                            scale=r2[:, qi_:qi_ + 1]),
                              waits=[tr2, tmr2, st_["ot_free"]], inc="s")
                t_add = rec.op("vector", lambda v, qi_=qi: v.tensor_add(
                    x2t[qi_][:], x2t[qi_][:], ot_bf[:]),
                    waits=[t_n2, x_toks[qi]], inc="v")
                st_["ot_free"] = t_add
                t_sum3 = rec.op("vector", lambda v, qi_=qi: v.reduce_sum(
                    stat[:, B3 + qi_:B3 + qi_ + 1], x2t[qi_][:], axis=AX.X),
                    waits=[t_add], inc="v")
                sq3_tok = rec.op("vector", lambda v, qi_=qi:
                                 v.scalar_tensor_tensor(
                                     out=scr_bf[:], in0=x2t[qi_][:],
                                     scalar=1.0, in1=x2t[qi_][:],
                                     op0=ALU.mult, op1=ALU.mult,
                                     accum_out=stat[:, B3 + 4 + qi_:B3 + 5 + qi_]),
                                 waits=[t_sum3], inc="v")
                yield
            r3, mr3, tr3, tmr3 = emit_stats(sq3_tok, sq3_tok, B3)
            t_n3 = None
            for qi in range(4):
                t_n3 = rec.op("scalar", lambda s2, qi_=qi:
                              s2.activation(ln3_tm[:], x2t[qi_][:],
                                            ACT.Identity,
                                            bias=mr3[:, qi_:qi_ + 1],
                                            scale=r3[:, qi_:qi_ + 1]),
                              waits=[tr3, tmr3, st_["ln3_tm_free"]],
                              inc="s")
                t_tp = transpose_group(
                    lambda c: ln3_tm[:, c * 128:(c + 1) * 128], 8,
                    [t_n3, st_["ln3_fm_free"]])
                st_["ln3_tm_free"] = t_tp
                ln3_ev = rec.op("vector", lambda v, qi_=qi: v.tensor_copy(
                    ln3_fm[:].rearrange("p (c x) -> p c x", x=512)
                    [:, :, qi_ * 128:(qi_ + 1) * 128],
                    pT[:].rearrange("p (c x) -> p c x", x=128)),
                    waits=[t_tp], inc="v")
                st_["pt_free"] = ln3_ev
                yield
            ln3_ready = ln3_ev
            st_["attn_tm_free"] = t_n3
            if dbg and g == 0:
                tapd("d_x2", x2t[0][:], st_["ot_free"])
                tapd("d_ln3", ln3_fm[:], ln3_ready)

            p_ready = None
            t_u1 = None
            for mh in range(2):
                par = mh % 2
                bank = pool[mh]
                t_wu1 = rec.dma("sync", "qw", wp_st[par][:],
                                wscr[:, OFF_U1 + mh * 1024:
                                     OFF_U1 + (mh + 1) * 1024],
                                waits=[st_["wp_free"][par],
                                       wtok_for(OFF_U1 + mh * 1024, 1024)])

                def f_u1(te, p=par, bk=bank):
                    last = None
                    for c in range(8):
                        last = mm(te, bk[:],
                                  wp_st[p][:, c * 128:(c + 1) * 128],
                                  ln3_fm[:, c * 512:(c + 1) * 512],
                                  c == 0, c == 7)
                    return last
                t_u1 = rec.op("tensor", f_u1,
                              waits=[t_wu1, ln3_ready, st_["pool_free"][mh]],
                              inc="t")
                st_["wp_free"][par] = t_u1
                p_ready = rec.op("vector",
                                 lambda v, mh_=mh, bk=bank: v.tensor_copy(
                                     P_bf[:, mh_ * 512:(mh_ + 1) * 512],
                                     bk[:]),
                                 waits=[t_u1], inc="v")
                st_["pool_free"][mh] = p_ready
                yield

            t_u2 = None
            for fc in range(NFC):
                par = fc % 2
                t_wv = rec.dma("sync", "qw", vu_st[par][:],
                               wscr[:, OFF_V1 + fc * 256:
                                    OFF_V1 + (fc + 1) * 256],
                               waits=[st_["vu_free"][par],
                                      wtok_for(OFF_V1 + fc * 256, 256)])
                t_wu = rec.dma("sync", "qw", u2_st[par][:],
                               wscr[:, OFF_U2 + fc * 256:
                                    OFF_U2 + (fc + 1) * 256],
                               waits=[st_["u2_free"][par],
                                      wtok_for(OFF_U2 + fc * 256, 256)])

                def f_v1(te, p=par):
                    mm(te, pC[:], vu_st[p][:, 0:128], P_bf[:, 0:512],
                       True, False)
                    return mm(te, pC[:], vu_st[p][:, 128:256],
                              P_bf[:, 512:1024], False, True)
                t_v1 = rec.op("tensor", f_v1,
                              waits=[t_wv, p_ready, st_["pool_free"][2]],
                              inc="t")
                st_["vu_free"][par] = t_v1
                t_g = rec.op("scalar", lambda s2, p=par, fc_=fc:
                             s2.activation(h_bf[p][:], pC[:],
                                           ACT.Gelu_apprx_tanh,
                                           bias=b1c[:, fc_:fc_ + 1]),
                             waits=[t_v1, st_["h_free"][par], t_bias],
                             inc="s")
                st_["pool_free"][2] = t_g

                def f_u2(te, p=par, fc_=fc):
                    mm(te, pA[:], u2_st[p][:, 0:128], h_bf[p][:],
                       fc_ == 0, fc_ == NFC - 1)
                    return mm(te, pB[:], u2_st[p][:, 128:256], h_bf[p][:],
                              fc_ == 0, fc_ == NFC - 1)
                w_u2 = [t_wu, t_g]
                if fc == 0:
                    w_u2 += [st_["pool_free"][0], st_["pool_free"][1]]
                t_u2 = rec.op("tensor", f_u2, waits=w_u2, inc="t")
                st_["u2_free"][par] = t_u2
                st_["h_free"][par] = t_u2

            def f_q2(v):
                v.tensor_copy(q2_bf[:, 0:512], pA[:])
                return v.tensor_copy(q2_bf[:, 512:1024], pB[:])
            t_q2 = rec.op("vector", f_q2,
                          waits=[t_u2, st_["q2_free"]], inc="v")
            st_["pool_free"][0] = t_q2
            st_["pool_free"][1] = t_q2
            yield
            if dbg and g == 0:
                tapd("d_P", P_bf[:], p_ready)
                tapd("d_q2", q2_bf[:], t_q2)

            t_v2 = None
            for qi in range(4):
                tch = s * 4 + qi
                b4_ = 96 + (qi % 2) * 16

                def f_v2(te, qi_=qi):
                    last = None
                    for seg in range(2):
                        bk = pool[seg]
                        for ch in range(2):
                            last = mm(te, bk[:],
                                      q2_bf[:, ch * 512 + qi_ * 128:
                                            ch * 512 + (qi_ + 1) * 128],
                                      v2_bf[:, ch * 1024 + seg * 512:
                                            ch * 1024 + (seg + 1) * 512],
                                      ch == 0, ch == 1)
                    return last
                t_v2 = rec.op("tensor", f_v2,
                              waits=[t_q2, st_["pool_free"][0],
                                     st_["pool_free"][1], cv_state["cv_tok"]], inc="t")

                def f_b2(v, b=b4_):
                    v.scalar_tensor_tensor(
                        out=o1[:, 0:512], in0=pA[:], scalar=1.0,
                        in1=b2r[:, 0:512], op0=ALU.mult, op1=ALU.add,
                        accum_out=stat[:, b + 6:b + 7])
                    return v.scalar_tensor_tensor(
                        out=o1[:, 512:1024], in0=pB[:], scalar=1.0,
                        in1=b2r[:, 512:1024], op0=ALU.mult, op1=ALU.add,
                        accum_out=stat[:, b + 7:b + 8])
                t_b2o = rec.op("vector", f_b2,
                               waits=[t_v2, t_bias, st_["o1_free"]],
                               inc="v")
                st_["pool_free"][0] = t_b2o
                st_["pool_free"][1] = t_b2o
                t_sum4 = rec.op("vector", lambda v, b=b4_: v.tensor_add(
                    stat[:, b:b + 1], stat[:, b + 6:b + 7],
                    stat[:, b + 7:b + 8]), waits=[t_b2o], inc="v")
                t_sq4 = rec.op("vector", lambda v, b=b4_:
                               v.scalar_tensor_tensor(
                                   out=scr_bf[:], in0=o1[:], scalar=1.0,
                                   in1=o1[:], op0=ALU.mult, op1=ALU.mult,
                                   accum_out=stat[:, b + 1:b + 2]),
                               waits=[t_b2o], inc="v")
                r4v, mr4v, tr4, tmr4 = emit_stats(t_sum4, t_sq4, b4_, nb=1)
                t_aff = rec.op("vector", lambda v, r_=r4v, m_=mr4v: v.tensor_scalar(
                    out=o1[:], in0=o1[:], scalar1=r_, scalar2=m_,
                    op0=ALU.mult, op1=ALU.add),
                    waits=[tr4, tmr4, t_sq4], inc="v")
                t_fin = rec.op("vector", lambda v, qi_=qi: v.tensor_add(
                    x2t[qi_][:], x2t[qi_][:], o1[:]), waits=[t_aff], inc="v")
                t_st = rec.dma("gpsimd", "qx", out_view[seq, tch, :, :],
                               x2t[qi][:], waits=[t_fin])
                st_["x2_free"][qi] = t_st
                yield
            st_["ln3_fm_free"] = t_u1

        def mk(kind, g):
            sq, sp = g // NSUP, g % NSUP
            if kind == "conv":
                return gen_conv()
            if kind == "p2":
                return gen_p2(sq, sp)
            if kind == "att":
                return gen_att(sq, sp)
            return gen_f(sq, sp)

        slots = [
            [("conv", 0), ("p2", 0)],
            [("att", 0), ("p2", 1)],
            [("att", 1), ("p2", 2), ("f", 0)],
            [("att", 2), ("p2", 3), ("f", 1)],
            [("att", 3), ("f", 2)],
            [("p2", 4), ("f", 3)],
            [("att", 4), ("p2", 5)],
            [("att", 5), ("p2", 6), ("f", 4)],
            [("att", 6), ("p2", 7), ("f", 5)],
            [("att", 7), ("f", 6)],
            [("f", 7)],
        ]
        if dbg:
            slots = ([[("p2", g)], [("att", g)], [("f", g)]]
                     for g in range(SEQS * NSUP))
            slots = [[("conv", 0)]] + [x for g in slots for x in g]
        for slot in slots:
            alive = [(k, g, mk(k, g)) for (k, g) in slot]
            while alive:
                for item in list(alive):
                    k, g, gn = item
                    try:
                        next(gn)
                    except StopIteration:
                        alive.remove(item)
                        if k == "att":
                            done_att[g] = True

        rec.op("gpsimd", lambda g: g.nop(), waits=[("qx", rec.cnt["qx"])])

        sems = {}
        for name in ("qw", "qx", "qg", "g", "v", "s", "t"):
            sems[name] = es.enter_context(nc.semaphore(name))

        block = es.enter_context(nc.Block())

        def replay(engname):
            def runner(eng):
                last_wait = {}
                for waits, inc_sem, amt, f in rec.q[engname]:
                    for (sn, val) in waits:
                        if last_wait.get(sn, -1) >= val:
                            continue
                        eng.wait_ge(sems[sn], val)
                        last_wait[sn] = val
                    ins = f(eng)
                    if inc_sem is not None and ins is not None:
                        ins.then_inc(sems[inc_sem], amt)
            return runner

        block.sync(replay("sync"))
        block.gpsimd(replay("gpsimd"))
        block.vector(replay("vector"))
        block.scalar(replay("scalar"))
        block.tensor(replay("tensor"))

    return nc


def make_images(inputs):
    f = np.float32
    W = np.asarray(inputs["L0_attn_w_attn"], f)
    b_attn = np.asarray(inputs["L0_attn_b_attn"], f)
    la = np.asarray(inputs["L0_attn_la_attn"], f)
    lb = np.asarray(inputs["L0_attn_lb_attn"], f)
    Wp = np.asarray(inputs["L0_attn_w_proj"], f)
    b_proj = np.asarray(inputs["L0_attn_b_proj"], f)
    lap = np.asarray(inputs["L0_attn_la_proj"], f)
    lbp = np.asarray(inputs["L0_attn_lb_proj"], f)
    U1 = np.asarray(inputs["L0_ffn_U1"], f)
    V1 = np.asarray(inputs["L0_ffn_V1"], f)
    b1 = np.asarray(inputs["L0_ffn_b1"], f)
    U2 = np.asarray(inputs["L0_ffn_U2"], f)
    V2 = np.asarray(inputs["L0_ffn_V2"], f)
    b2 = np.asarray(inputs["L0_ffn_b2"], f)

    qkv = W.T.reshape(8, 128, 3072).transpose(1, 0, 2)
    m_slices = [np.ascontiguousarray(qkv[:, :, m * 128:(m + 1) * 128]
                                     .reshape(128, 1024)) for m in range(24)]
    la_img = np.ascontiguousarray(la.T.reshape(8, 128, 32).transpose(1, 0, 2)
                                  .reshape(128, 256))
    wpT = Wp.T.reshape(8, 128, 1024).transpose(1, 0, 2)
    wp_slices = [np.ascontiguousarray(wpT[:, :, m * 128:(m + 1) * 128]
                                      .reshape(128, 1024)) for m in range(8)]
    v1r = V1.reshape(2, 128, 4096).transpose(1, 0, 2)
    v1_slices = [np.ascontiguousarray(v1r[:, :, fc * 128:(fc + 1) * 128]
                                      .reshape(128, 256)) for fc in range(32)]
    u2r = U2.reshape(32, 128, 256).transpose(1, 0, 2)
    u2_slices = [np.ascontiguousarray(u2r[:, fc, :].reshape(128, 256))
                 for fc in range(32)]
    u1r = U1.reshape(8, 128, 256).transpose(1, 0, 2)
    u1_slices = [np.ascontiguousarray(u1r[:, :, mh * 128:(mh + 1) * 128]
                                      .reshape(128, 1024)) for mh in range(2)]
    wbig = np.ascontiguousarray(
        np.concatenate(m_slices + [la_img] + wp_slices + v1_slices + u2_slices
                       + u1_slices, axis=1))
    assert wbig.shape == (128, WCOLS), wbig.shape

    v2_img = V2.reshape(2, 128, 1024).transpose(1, 0, 2).reshape(128, 2048)
    lap_img = lap.T.reshape(8, 128, 32).transpose(1, 0, 2).reshape(128, 256)
    wsmall = np.ascontiguousarray(
        np.concatenate([v2_img, lap_img], axis=1))

    wlb = np.zeros((32, 4096), f)
    wlb[:, 0:3072] = lb.T
    wlb[:, 3072:4096] = lbp.T

    return dict(
        wbig=wbig, wsmall=wsmall, wlb=wlb,
        bqk=np.ascontiguousarray(b_attn.reshape(24, 128).T),
        b1c=np.ascontiguousarray(b1.reshape(32, 128).T),
        bp=np.ascontiguousarray(b_proj.reshape(8, 128).T),
        b2r=np.ascontiguousarray(np.broadcast_to(b2[None, :], (128, 1024))),
    )


def kernel(**inputs):
    imgs = make_images(inputs)
    x = np.asarray(inputs["L0_attn_x"], np.float32)
    nc = build_nc()
    in_maps = []
    for c in range(NCORE):
        m = dict(imgs)
        m["x"] = np.ascontiguousarray(x[SEQS * c:SEQS * (c + 1)])
        in_maps.append(m)
    res = run_bass_kernel_spmd(nc, in_maps, core_ids=list(range(NCORE)))
    out = np.empty_like(x)
    for c in range(NCORE):
        out[SEQS * c:SEQS * (c + 1)] = res.results[c]["out"]
    return out


# version 24
# speedup vs baseline: 1.2716x; 1.2716x over previous
import numpy as np
from contextlib import ExitStack

import concourse.bass as bass
import concourse.mybir as mybir
from concourse.bass_utils import run_bass_kernel_spmd

F32 = mybir.dt.float32
BF16 = mybir.dt.bfloat16
AX = mybir.AxisListType
ALU = mybir.AluOpType
ACT = mybir.ActivationFunctionType

NCORE = 8
SEQS = 2
T = 2048
C = 1024
NSUP = 4
NFC = 32

OFF_QKV = 0
OFF_LA = 24 * 1024
OFF_WP = OFF_LA + 256
OFF_V1 = OFF_WP + 8192
OFF_U2 = OFF_V1 + 8192
OFF_U1 = OFF_U2 + 8192
WCOLS = OFF_U1 + 2048

ENGS = ("sync", "gpsimd", "vector", "scalar", "tensor")


class Rec:

    def __init__(self):
        self.q = {e: [] for e in ENGS}
        self.cnt = {}
        self.own = {"vector": "v", "scalar": "s", "tensor": "t", "gpsimd": "g",
                    "sync": None}

    def sem(self, name):
        self.cnt[name] = 0

    def op(self, eng, f, waits=(), inc=None, amt=1):
        ws = tuple(w for w in waits if w is not None)
        tok = None
        if inc is not None:
            self.cnt[inc] += amt
            tok = (inc, self.cnt[inc])
        self.q[eng].append((ws, inc, amt, f))
        return tok

    def dma(self, eng, qname, out_ap, in_ap, waits=()):
        ws = tuple(w for w in waits if w is not None)
        self.cnt[qname] += 16
        tok = (qname, self.cnt[qname])

        def f(e, _o=out_ap, _i=in_ap):
            return e.dma_start(out=_o, in_=_i)

        self.q[eng].append((ws, qname, 16, f))
        return tok


def build_nc(dbg=None):
    nc = bass.Bass("TRN2", target_bir_lowering=False)

    x_h = nc.dram_tensor("x", [SEQS, T, C], F32, kind="ExternalInput").ap()
    wbig = nc.dram_tensor("wbig", [128, WCOLS], F32, kind="ExternalInput").ap()
    wsmall = nc.dram_tensor("wsmall", [128, 2304], F32, kind="ExternalInput").ap()
    wlb = nc.dram_tensor("wlb", [32, 4096], F32, kind="ExternalInput").ap()
    bqk_h = nc.dram_tensor("bqk", [128, 24], F32, kind="ExternalInput").ap()
    b1c_h = nc.dram_tensor("b1c", [128, 32], F32, kind="ExternalInput").ap()
    bp_h = nc.dram_tensor("bp", [128, 8], F32, kind="ExternalInput").ap()
    b2r_h = nc.dram_tensor("b2r", [128, 1024], F32, kind="ExternalInput").ap()
    out_h = nc.dram_tensor("out", [SEQS, T, C], F32, kind="ExternalOutput").ap()
    dbg_t = {}
    if dbg:
        for name, shape in (("d_ln1", [128, 4096]), ("d_q", [128, 4096]),
                            ("d_k", [128, 8 * T]), ("d_v", [128, 16 * 1040]),
                            ("d_plq", [32, 512]), ("d_y", [128, 4096]),
                            ("d_yacc", [128, 520]),
                            ("d_attn", [128, 4096]), ("d_x2", [128, 1024]),
                            ("d_ln3", [128, 4096]), ("d_P", [128, 1024]),
                            ("d_q2", [128, 1024]), ("d_yfm", [128, 4096])):
            if name in dbg.get("taps", ()):
                dbg_t[name] = nc.dram_tensor(name, shape, F32, kind="ExternalOutput").ap()
    wscr = nc.dram_tensor("wscr", [128, WCOLS], BF16, kind="Internal").ap()

    x_view = x_h.rearrange("s (n p) c -> s n p c", p=128)
    out_view = out_h.rearrange("s (n p) c -> s n p c", p=128)

    es = ExitStack()
    SB = lambda n, s, d: es.enter_context(nc.sbuf_tensor(n, s, d))
    PS = lambda n, s, d: es.enter_context(nc.psum_tensor(n, s, d))

    with es:
        iden = SB("iden", [128, 128], BF16)
        tri = SB("tri", [128, 128], BF16)
        cm20 = SB("cm20", [128, 1], F32)
        ceps = SB("ceps", [128, 1], F32)
        lbq_bf = SB("lbq_bf", [32, 3072], BF16)
        lbp_bf = SB("lbp_bf", [32, 1024], BF16)
        lap_bf = SB("lap_bf", [128, 256], BF16)
        v2_bf = SB("v2_bf", [128, 2048], BF16)
        bqk = SB("bqk_s", [128, 24], F32)
        b1c = SB("b1c_s", [128, 32], F32)
        bp = SB("bp_s", [128, 8], F32)
        b2r = SB("b2r_s", [128, 1024], F32)
        k_fm = SB("k_fm", [128, 8 * T], BF16)
        v_tm = SB("v_tm", [128, 16 * 1040], BF16)
        q_sup = SB("q_sup", [128, 4096], BF16)
        y_sup2 = [SB(f"y_sup{i}", [128, 4096], BF16) for i in range(2)]
        e_bf = [SB(f"e_bf{i}", [128, 1024], BF16) for i in range(4)]
        y_acc = SB("y_acc", [128, 520], F32)
        r8 = SB("r8", [128, 8], F32)
        ln1_fm = SB("ln1_fm", [128, 4096], BF16)
        ln1_tm = SB("ln1_tm", [128, 1024], BF16)
        ln3_tm = SB("ln3_tm", [128, 1024], BF16)
        ln3_fm = SB("ln3_fm", [128, 4096], BF16)
        y_fm = SB("y_fm", [128, 4096], BF16)
        attn_tm = SB("attn_tm", [128, 4096], BF16)
        vf_bf = [SB(f"vf_bf{i}", [128, 512], BF16) for i in range(2)]
        po_bf = [SB(f"po_bf{i}", [128, 512], BF16) for i in range(2)]
        P_bf = SB("P_bf", [128, 1024], BF16)
        h_bf = [SB(f"h_bf{i}", [128, 512], BF16) for i in range(2)]
        q2_bf = SB("q2_bf", [128, 1024], BF16)
        plq_bf = SB("plq_bf", [32, 512], BF16)
        plp_bf = SB("plp_bf", [32, 512], BF16)
        ot_bf = SB("ot_bf", [128, 1024], BF16)
        scr_bf = SB("scr_bf", [128, 1024], BF16)
        o1 = SB("o1", [128, 1024], F32)
        stat = SB("stat", [128, 128], F32)
        xa = [SB(f"xa{i}", [128, 1024], F32) for i in range(2)]
        x2t = [SB(f"x2t{i}", [128, 1024], F32) for i in range(4)]
        wq_st = [SB(f"wq_st{i}", [128, 1024], BF16) for i in range(2)]
        wp_st = [SB(f"wp_st{i}", [128, 1024], BF16) for i in range(2)]
        vu_st = [SB(f"vu_st{i}", [128, 256], BF16) for i in range(2)]
        u2_st = [SB(f"u2_st{i}", [128, 256], BF16) for i in range(2)]

        pA = PS("pA", [128, 512], F32)
        pB = PS("pB", [128, 512], F32)
        pC = PS("pC", [128, 512], F32)
        pAtt = PS("pAtt", [128, 1024], F32)
        pY0 = PS("pY0", [128, 512], F32)
        pY1 = PS("pY1", [128, 512], F32)
        pT = PS("pT", [128, 1024], BF16)
        pool = [pA, pB, pC]

        rec = Rec()
        for s_ in ("qw", "qx", "qs", "g", "v", "s", "t"):
            rec.sem(s_)

        st_ = {
            "pt_free": None, "pool_free": [None, None, None],
            "xa_free": [None, None], "wq_free": [None, None],
            "wp_free": [None, None], "ysup_free": [None, None],
            "vu_free": [None, None], "u2_free": [None, None],
            "vf_free": [None, None], "po_free": [None, None],
            "h_free": [None, None], "x2_free": [None] * 4,
            "ebf_free": [None] * 4, "pY_free": None,
            "attexp_last": None, "ln1_tm_free": None, "ln3_tm_free": None,
            "ln1_fm_free": None, "ln3_fm_free": None, "y_fm_free": None,
            "attn_tm_free": None, "qsup_free": None,
            "kv_free": None, "ot_free": None,
            "o1_free": None, "q2_free": None,
        }

        def gp_consts(g):
            g.memset(iden[:], 1.0)
            g.affine_select(out=iden[:], in_=iden[:], compare_op=ALU.is_equal,
                            fill=0.0, base=0, pattern=[[-1, 128]],
                            channel_multiplier=1)
            g.memset(tri[:], 0.0)
            g.affine_select(out=tri[:], in_=tri[:], compare_op=ALU.is_gt,
                            fill=1.0, base=0, pattern=[[-1, 128]],
                            channel_multiplier=1)
            g.memset(cm20[:], -20.0)
            g.memset(ceps[:], 1e-5)
            return g.memset(
                v_tm[:].rearrange("p (k h d) -> p k h d", h=16, d=65)[:, :, :, 64:65],
                1.0)

        t_consts = rec.op("gpsimd", gp_consts, inc="g")

        t_bias = None
        for dst, src in ((bqk, bqk_h), (b1c, b1c_h), (bp, bp_h), (b2r, b2r_h)):
            t_bias = rec.dma("sync", "qw", dst[:], src)

        wtok = {}
        cv_state = {"cv_tok": None}

        CONV_CHUNKS = (
            [(OFF_LA, 256, "la")]
            + [(m * 1024, 1024, f"qkv{m}") for m in range(16, 24)]
            + [(m * 1024, 1024, f"qkv{m}") for m in range(8, 16)]
            + [(m * 1024, 1024, f"qkv{m}") for m in range(8)]
            + [(OFF_WP + m * 1024, 1024, f"wp{m}") for m in range(8)]
            + [(OFF_U1 + m * 1024, 1024, f"u1{m}") for m in range(2)]
            + [x for c in range(8) for x in
               ((OFF_V1 + c * 1024, 1024, f"v1{c}"),
                (OFF_U2 + c * 1024, 1024, f"u2{c}"))]
        )

        def wtok_for(off, width):
            for (o, w, key) in CONV_CHUNKS:
                if o <= off and off + width <= o + w:
                    assert key in wtok, (off, width, key)
                    return wtok[key]
            raise AssertionError((off, width))

        def gen_conv():
            smalls = [
                (v2_bf[:, 0:1024], wsmall[:, 0:1024], 128),
                (v2_bf[:, 1024:2048], wsmall[:, 1024:2048], 128),
                (lap_bf[:, 0:256], wsmall[:, 2048:2304], 128),
                (lbq_bf[:, 0:1024], wlb[0:32, 0:1024], 32),
                (lbq_bf[:, 1024:2048], wlb[0:32, 1024:2048], 32),
                (lbq_bf[:, 2048:3072], wlb[0:32, 2048:3072], 32),
                (lbp_bf[:, 0:1024], wlb[0:32, 3072:4096], 32),
            ]
            fst = [x2t[0], x2t[1]]
            fguard = [None, None]
            for i, (dst, srcap, np_) in enumerate(smalls):
                par = i % 2
                nf = srcap.shape[-1]
                t_ld = rec.dma("sync", "qw", fst[par][0:np_, 0:nf], srcap,
                               waits=[fguard[par]])
                tok = rec.op(
                    "vector",
                    lambda v, d=dst, p=par, n0=np_, n1=nf: v.tensor_copy(
                        d, fst[p][0:n0, 0:n1]),
                    waits=[t_ld], inc="v")
                fguard[par] = tok
                cv_state["cv_tok"] = tok
                if i % 2 == 1:
                    yield
            fbig = [o1, x2t[2]]
            bbig = [q2_bf, ot_bf]
            fg = [None, None]
            bg = [None, None]
            for i, (off, w, key) in enumerate(CONV_CHUNKS):
                par = i % 2
                t_ld = rec.dma("sync", "qw", fbig[par][:, 0:w],
                               wbig[:, off:off + w], waits=[fg[par]])
                t_cv = rec.op(
                    "vector",
                    lambda v, p=par, w_=w: v.tensor_copy(bbig[p][:, 0:w_],
                                                         fbig[p][:, 0:w_]),
                    waits=[t_ld, bg[par]], inc="v")
                fg[par] = t_cv
                t_st = rec.dma("scalar", "qs", wscr[:, off:off + w],
                               bbig[par][:, 0:w], waits=[t_cv])
                bg[par] = t_st
                wtok[key] = t_st
                if i % 2 == 1:
                    yield
            st_["x2_free"] = [fguard[0], fguard[1], fg[1], None]
            st_["o1_free"] = fg[0]
            st_["q2_free"] = bg[0]
            st_["ot_free"] = bg[1]

        def mm(te, out_ap, lh, rh, start, stop, tp=None):
            return te.matmul(out_ap, lh, rh, start=start, stop=stop,
                             tile_position=tp)

        def transpose_group(src_fn, nblk, waits):
            def f(te):
                last = None
                for c in range(nblk):
                    last = te.transpose(pT[:, c * 128:(c + 1) * 128],
                                        src_fn(c), iden[:])
                return last
            return rec.op("tensor", f,
                          waits=list(waits) + [st_["pt_free"], t_consts],
                          inc="t")

        def emit_stats(t_sum, t_sq, base, nb=4):
            c = lambda j: stat[:, base + j * nb: base + (j + 1) * nb]
            t1 = rec.op("scalar", lambda s2: s2.activation(
                c(2), c(0), ACT.Identity, scale=1.0 / 1024), waits=[t_sum], inc="s")
            t2 = rec.op("vector", lambda v: v.scalar_tensor_tensor(
                out=c(3), in0=c(2), scalar=-1.0, in1=c(2),
                op0=ALU.mult, op1=ALU.mult), waits=[t1], inc="v")
            t2b = rec.op("vector", lambda v: v.tensor_scalar(
                out=c(6), in0=c(1), scalar1=1.0 / 1024, scalar2=ceps[:],
                op0=ALU.mult, op1=ALU.add), waits=[t_sq, t_consts], inc="v")
            t2c = rec.op("vector", lambda v: v.tensor_add(c(6), c(6), c(3)),
                         waits=[t2, t2b], inc="v")
            t3 = rec.op("scalar", lambda s2: s2.activation(
                c(7), c(6), ACT.Ln), waits=[t2c], inc="s")
            t4 = rec.op("scalar", lambda s2: s2.activation(
                c(4), c(7), ACT.Exp, scale=-0.5), waits=[t3], inc="s")
            t5 = rec.op("vector", lambda v: v.scalar_tensor_tensor(
                out=c(5), in0=c(2), scalar=-1.0, in1=c(4),
                op0=ALU.mult, op1=ALU.mult), waits=[t4], inc="v")
            return c(4), c(5), t4, t5

        def tapd(name, buf_ap, tok):
            if not (dbg and name in dbg_t):
                return
            w = buf_ap.shape[-1]
            npart = buf_ap.shape[0]
            done = 0
            while done < w:
                seg = min(1024, w - done)
                t_cv = rec.op("vector",
                              lambda v, b=buf_ap, d0=done, sg=seg, np_=npart:
                              v.tensor_copy(o1[0:np_, 0:sg], b[0:np_, d0:d0 + sg]),
                              waits=[tok, st_.get("dbg_free")], inc="v")
                st_["dbg_free"] = rec.dma("gpsimd", "qx",
                                          dbg_t[name][0:npart, done:done + seg],
                                          o1[0:npart, 0:seg], waits=[t_cv])
                done += seg

        seqd = {sq: {"k": [], "v": []} for sq in range(SEQS)}
        sup_tok = {}
        done_att = {}

        def gen_p2(seq, s):
            g = seq * NSUP + s
            ln1_ev = None
            for tl in range(4):
                tch = s * 4 + tl
                par = tl % 2
                base = par * 8
                t_x = rec.dma("gpsimd", "qx", xa[par][:],
                              x_view[seq, tch, :, :],
                              waits=[st_["xa_free"][par]])
                t_sum = rec.op("vector", lambda v, p=par, b=base: v.reduce_sum(
                    stat[:, b:b + 1], xa[p][:], axis=AX.X),
                    waits=[t_x], inc="v")
                t_sq = rec.op("vector", lambda v, p=par, b=base: v.scalar_tensor_tensor(
                    out=scr_bf[:], in0=xa[p][:], scalar=1.0, in1=xa[p][:],
                    op0=ALU.mult, op1=ALU.mult,
                    accum_out=stat[:, b + 1:b + 2]),
                    waits=[t_sum], inc="v")
                r1, mr1, tr, tmr = emit_stats(t_sum, t_sq, base, nb=1)
                t_n = rec.op("scalar", lambda s2, p=par, r_=r1, m_=mr1:
                             s2.activation(ln1_tm[:], xa[p][:], ACT.Identity,
                                           bias=m_, scale=r_),
                             waits=[tr, tmr, st_["ln1_tm_free"]], inc="s")
                st_["xa_free"][par] = t_n
                t_tp = transpose_group(
                    lambda c: ln1_tm[:, c * 128:(c + 1) * 128], 8,
                    [t_n, st_["ln1_fm_free"]])
                st_["ln1_tm_free"] = t_tp
                ln1_ev = rec.op("vector", lambda v, tl_=tl: v.tensor_copy(
                    ln1_fm[:].rearrange("p (c x) -> p c x", x=512)
                    [:, :, tl_ * 128:(tl_ + 1) * 128],
                    pT[:].rearrange("p (c x) -> p c x", x=128)),
                    waits=[t_tp], inc="v")
                st_["pt_free"] = ln1_ev
                yield
            ln1_ready = ln1_ev
            if dbg and g == 0:
                tapd("d_ln1", ln1_fm[:], ln1_ready)

            t_la = rec.dma("sync", "qw", wq_st[0][:, 0:256],
                           wscr[:, OFF_LA:OFF_LA + 256],
                           waits=[st_["wq_free"][0], wtok_for(OFF_LA, 256)])

            def f_la(te):
                last = None
                for c in range(8):
                    last = mm(te, pC[0:32, :],
                              wq_st[0][:, c * 32:(c + 1) * 32],
                              ln1_fm[:, c * 512:(c + 1) * 512],
                              c == 0, c == 7)
                return last
            t_lam = rec.op("tensor", f_la,
                           waits=[t_la, ln1_ready, st_["pool_free"][2],
                                  cv_state["cv_tok"]], inc="t")
            t_plq = rec.op("scalar", lambda s2: s2.activation(
                plq_bf[:, :], pC[0:32, :], ACT.Identity, scale=0.5),
                waits=[t_lam], inc="s")
            st_["pool_free"][2] = t_plq
            st_["wq_free"][0] = t_lam
            yield

            q_ready = None
            t_mm = None
            morder = list(range(16, 24)) + list(range(8, 16)) + list(range(8))
            for mi, m in enumerate(morder):
                if m == 0 and g > 0:
                    while not done_att.get(g - 1, False):
                        yield
                par = mi % 2
                bi = mi % 2
                bank = pool[bi]
                t_w = rec.dma("sync", "qw", wq_st[par][:],
                              wscr[:, OFF_QKV + m * 1024:
                                   OFF_QKV + (m + 1) * 1024],
                              waits=[st_["wq_free"][par], wtok_for(OFF_QKV + m * 1024, 1024)])

                def f_m(te, p=par, bk=bank, m_=m):
                    last = None
                    for c in range(8):
                        last = mm(te, bk[:],
                                  wq_st[p][:, c * 128:(c + 1) * 128],
                                  ln1_fm[:, c * 512:(c + 1) * 512],
                                  c == 0, False)
                    return mm(te, bk[:],
                              lbq_bf[0:32, m_ * 128:(m_ + 1) * 128],
                              plq_bf[0:32, :], False, True)
                t_mm = rec.op("tensor", f_m,
                              waits=[t_w, ln1_ready, t_plq,
                                     st_["pool_free"][bi], cv_state["cv_tok"]],
                              inc="t")
                st_["wq_free"][par] = t_mm
                if m < 8:
                    t_ev = rec.op("scalar",
                                  lambda s2, bk=bank, m_=m: s2.activation(
                                      q_sup[:, m_ * 512:(m_ + 1) * 512],
                                      bk[:], ACT.Identity,
                                      bias=bqk[:, m_:m_ + 1]),
                                  waits=[t_mm, t_bias, st_["qsup_free"]],
                                  inc="s")
                    st_["pool_free"][bi] = t_ev
                    if m == 7:
                        q_ready = t_ev
                elif m < 16:
                    mk = m - 8
                    t_ev = rec.op("scalar",
                                  lambda s2, bk=bank, mk_=mk, m_=m, s_=s:
                                  s2.activation(
                                      k_fm[:, mk_ * 2048 + s_ * 512:
                                           mk_ * 2048 + (s_ + 1) * 512],
                                      bk[:], ACT.Identity,
                                      bias=bqk[:, m_:m_ + 1]),
                                  waits=[t_mm, t_bias, st_["kv_free"]],
                                  inc="s")
                    st_["pool_free"][bi] = t_ev
                    if m == 15:
                        seqd[seq]["k"].append(t_ev)
                else:
                    mv = m - 16
                    vp = mi % 2
                    t_ev = rec.op("scalar",
                                  lambda s2, bk=bank, m_=m, vp_=vp:
                                  s2.activation(vf_bf[vp_][:], bk[:],
                                                ACT.Identity,
                                                bias=bqk[:, m_:m_ + 1]),
                                  waits=[t_mm, t_bias,
                                         st_["vf_free"][vp]], inc="s")
                    st_["pool_free"][bi] = t_ev
                    t_tp = transpose_group(
                        lambda c, vp_=vp: vf_bf[vp_][:, c * 128:(c + 1) * 128],
                        4, [t_ev, st_["kv_free"]])
                    st_["vf_free"][vp] = t_tp
                    t_vev = rec.op("vector", lambda v, mv_=mv, s_=s: v.tensor_copy(
                        v_tm[:].rearrange("p (k h d) -> p k h d", h=16, d=65)
                        [:, s_ * 4:(s_ + 1) * 4, 2 * mv_:2 * mv_ + 2, 0:64],
                        pT[:, 0:512].rearrange("p (k hd) -> p k hd", hd=128)
                        .rearrange("p k (h d) -> p k h d", d=64)),
                        waits=[t_tp], inc="v")
                    st_["pt_free"] = t_vev
                    if m == 23:
                        seqd[seq]["v"].append(t_vev)
                yield
            st_["ln1_fm_free"] = t_mm
            sup_tok[g] = {"q": q_ready}
            if dbg and g == 0:
                tapd("d_q", q_sup[:], q_ready)
                tapd("d_k", k_fm[:], q_ready)
                tapd("d_v", v_tm[:], q_ready)
                tapd("d_plq", plq_bf[:], t_plq)

        def gen_att(seq, s):
            g = seq * NSUP + s
            y_sup = y_sup2[g % 2]
            q_ready = sup_tok[g]["q"]
            k_list = seqd[seq]["k"]
            v_list = seqd[seq]["v"]
            att_pv_last = None
            norm_last = None
            for pair in range(8):
                chunk_add = None
                for kc in range(s + 1):
                    exp3 = None
                    mask_tok = None
                    for kk in range(4):
                        ki = kc * 4 + kk
                        q0 = kk * 128 if kc == s else 0

                        def f_qk(te, pr=pair, ki_=ki, q0_=0):
                            mm(te, pAtt[:, q0_:512],
                               k_fm[0:64, pr * 2048 + ki_ * 128:
                                    pr * 2048 + (ki_ + 1) * 128],
                               q_sup[0:64, pr * 512 + q0_:(pr + 1) * 512],
                               True, True, tp=(0, 0))
                            return mm(te, pAtt[:, 512 + q0_:1024],
                                      k_fm[64:128, pr * 2048 + ki_ * 128:
                                           pr * 2048 + (ki_ + 1) * 128],
                                      q_sup[64:128, pr * 512 + q0_:(pr + 1) * 512],
                                      True, True, tp=(64, 0))
                        t_qk = rec.op("tensor", f_qk,
                                      waits=[st_["attexp_last"], q_ready,
                                             k_list[kc]], inc="t")
                        t_exp = rec.op("scalar", lambda s2, kk_=kk, q0_=q0:
                                       s2.activation(
                                           e_bf[kk_][:].rearrange(
                                               "p (h x) -> p h x", x=512)[:, :, q0_:512],
                                           pAtt[:].rearrange(
                                               "p (h x) -> p h x", x=512)[:, :, q0_:512],
                                           ACT.Exp, scale=0.125,
                                           bias=cm20[:]),
                                       waits=[t_qk], inc="s")
                        st_["attexp_last"] = t_exp
                        exp3 = t_exp
                        yield
                    if kc == s:
                        def f_mask(v):
                            last = None
                            for kk in range(4):
                                sl_ap = (e_bf[kk][:]
                                         .rearrange("p (h x) -> p h x", x=512)
                                         [:, :, kk * 128:(kk + 1) * 128])
                                last = v.tensor_tensor(
                                    out=sl_ap, in0=sl_ap,
                                    in1=tri[:].unsqueeze(1)
                                    .broadcast_to([128, 2, 128]),
                                    op=ALU.mult)
                            return last
                        mask_tok = rec.op("vector", f_mask,
                                          waits=[exp3, t_consts], inc="v")

                    first = kc == 0
                    t_pv = None
                    for half in range(2):
                        def f_pv(te, kc_=kc, pr=pair, s_=s, hf=half):
                            last = None
                            for sl in range(hf * 4, hf * 4 + 4):
                                qi_loc, hh = sl // 2, sl % 2
                                kmax = 3 if kc_ < s_ else qi_loc
                                bank = pY0 if hf == 0 else pY1
                                o = (sl % 4) * 128
                                for kk in range(kmax + 1):
                                    ki = kc_ * 4 + kk
                                    last = mm(
                                        te, bank[:, o:o + 65],
                                        e_bf[kk][:, hh * 512 + qi_loc * 128:
                                                 hh * 512 + (qi_loc + 1) * 128],
                                        v_tm[:, ki * 1040 + (2 * pr + hh) * 65:
                                             ki * 1040 + (2 * pr + hh) * 65 + 65],
                                        kk == 0, kk == kmax)
                            return last
                        t_pv = rec.op("tensor", f_pv,
                                      waits=[exp3, mask_tok, v_list[kc],
                                             st_["pY_free"], t_consts], inc="t")

                        def f_acc(v, fr=first, hf=half):
                            d = y_acc[:].rearrange("p (sl c) -> p sl c", c=65)
                            src = (pY0 if hf == 0 else pY1)[:].rearrange(
                                "p (sl c) -> p sl c", c=128)[:, :, 0:65]
                            dd = d[:, hf * 4:hf * 4 + 4, :]
                            if fr:
                                return v.tensor_copy(dd, src)
                            return v.tensor_add(dd, dd, src)
                        chunk_add = rec.op("vector", f_acc, waits=[t_pv],
                                           inc="v")
                        yield
                    att_pv_last = t_pv
                    st_["pY_free"] = chunk_add

                t_rec = rec.op("vector", lambda v: v.reciprocal(
                    r8[:],
                    y_acc[:].rearrange("p (sl c) -> p sl c", c=65)
                    [:, :, 64:65].rearrange("p a b -> p (a b)")),
                    waits=[chunk_add], inc="v")

                def f_norm(v, pr=pair, ys=y_sup):
                    return v.tensor_tensor(
                        out=ys[:].rearrange("p (q rest) -> p q rest",
                                            rest=1024)
                        [:, :, pr * 128:(pr + 1) * 128]
                        .rearrange("p q (h d) -> p q h d", d=64),
                        in0=y_acc[:].rearrange("p (q h c) -> p q h c",
                                               h=2, c=65)[:, :, :, 0:64],
                        in1=r8[:].rearrange("p (a b) -> p a b", b=2)
                        .unsqueeze(3).broadcast_to([128, 4, 2, 64]),
                        op=ALU.mult)
                norm_last = rec.op("vector", f_norm,
                                   waits=[t_rec, st_["ysup_free"][g % 2]],
                                   inc="v")
                yield
            st_["qsup_free"] = att_pv_last
            sup_tok[g]["y"] = norm_last
            if s == NSUP - 1:
                st_["kv_free"] = att_pv_last
            if dbg and g == 0:
                tapd("d_y", y_sup[:], norm_last)
                tapd("d_yacc", y_acc[:], norm_last)

        def gen_f(seq, s):
            g = seq * NSUP + s
            y_sup = y_sup2[g % 2]
            norm_last = sup_tok[g]["y"]
            yfm_ev = None
            t_tp = None
            for qi in range(4):
                t_tp = transpose_group(
                    lambda c, qi_=qi: y_sup[:, qi_ * 1024 + c * 128:
                                            qi_ * 1024 + (c + 1) * 128],
                    8, [norm_last, st_["y_fm_free"]])
                yfm_ev = rec.op("vector", lambda v, qi_=qi: v.tensor_copy(
                    y_fm[:].rearrange("p (c x) -> p c x", x=512)
                    [:, :, qi_ * 128:(qi_ + 1) * 128],
                    pT[:].rearrange("p (c x) -> p c x", x=128)),
                    waits=[t_tp], inc="v")
                st_["pt_free"] = yfm_ev
                yield
            st_["ysup_free"][g % 2] = t_tp

            def f_lp(te):
                last = None
                for c in range(8):
                    last = mm(te, pC[0:32, :],
                              lap_bf[:, c * 32:(c + 1) * 32],
                              y_fm[:, c * 512:(c + 1) * 512], c == 0, c == 7)
                return last
            t_lpm = rec.op("tensor", f_lp,
                           waits=[yfm_ev, st_["pool_free"][2], cv_state["cv_tok"]],
                           inc="t")
            t_plp = rec.op("scalar", lambda s2: s2.activation(
                plp_bf[:, :], pC[0:32, :], ACT.Identity, scale=0.5),
                waits=[t_lpm], inc="s")
            st_["pool_free"][2] = t_plp
            yield

            last_proj_mm = None
            attn_ev = None
            for m in range(8):
                par = m % 2
                bi = m % 2
                bank = pool[bi]
                t_w = rec.dma("sync", "qw", wp_st[par][:],
                              wscr[:, OFF_WP + m * 1024:
                                   OFF_WP + (m + 1) * 1024],
                              waits=[st_["wp_free"][par], wtok_for(OFF_WP + m * 1024, 1024)])

                def f_pm(te, p=par, bk=bank, m_=m):
                    last = None
                    for c in range(8):
                        last = mm(te, bk[:],
                                  wp_st[p][:, c * 128:(c + 1) * 128],
                                  y_fm[:, c * 512:(c + 1) * 512],
                                  c == 0, False)
                    return mm(te, bk[:],
                              lbp_bf[0:32, m_ * 128:(m_ + 1) * 128],
                              plp_bf[0:32, :], False, True)
                t_mm2 = rec.op("tensor", f_pm,
                               waits=[t_w, yfm_ev, t_plp, cv_state["cv_tok"],
                                      st_["pool_free"][bi]], inc="t")
                st_["wp_free"][par] = t_mm2
                last_proj_mm = t_mm2
                t_ev = rec.op("scalar",
                              lambda s2, bk=bank, m_=m, p=par: s2.activation(
                                  po_bf[p][:], bk[:], ACT.Identity,
                                  bias=bp[:, m_:m_ + 1]),
                              waits=[t_mm2, t_bias, st_["po_free"][par]],
                              inc="s")
                st_["pool_free"][bi] = t_ev
                t_tp = transpose_group(
                    lambda c, p=par: po_bf[p][:, c * 128:(c + 1) * 128], 4,
                    [t_ev])
                st_["po_free"][par] = t_tp
                attn_ev = rec.op("vector", lambda v, m_=m: v.tensor_copy(
                    attn_tm[:].rearrange("p (q x) -> p q x", x=1024)
                    [:, :, m_ * 128:(m_ + 1) * 128],
                    pT[:, 0:512].rearrange("p (q x) -> p q x", x=128)),
                    waits=[t_tp, st_["attn_tm_free"]], inc="v")
                st_["pt_free"] = attn_ev
                yield
            st_["y_fm_free"] = last_proj_mm
            attn_ready = attn_ev
            if dbg and g == 0:
                tapd("d_yfm", y_fm[:], yfm_ev)
                tapd("d_attn", attn_tm[:], attn_ev)

            ln3_ev = None
            B2, B3 = 32, 64
            sq2_tok = None
            x_toks = []
            for qi in range(4):
                tch = s * 4 + qi
                a_sl = attn_tm[:, qi * 1024:(qi + 1) * 1024]
                t_x = rec.dma("gpsimd", "qx", x2t[qi][:],
                              x_view[seq, tch, :, :],
                              waits=[st_["x2_free"][qi]])
                x_toks.append(t_x)
                t_sum = rec.op("vector", lambda v, qi_=qi, a=a_sl: v.reduce_sum(
                    stat[:, B2 + qi_:B2 + qi_ + 1], a, axis=AX.X),
                    waits=[attn_ready], inc="v")
                sq2_tok = rec.op("vector", lambda v, qi_=qi, a=a_sl:
                                 v.scalar_tensor_tensor(
                                     out=scr_bf[:], in0=a, scalar=1.0, in1=a,
                                     op0=ALU.mult, op1=ALU.mult,
                                     accum_out=stat[:, B2 + 4 + qi_:B2 + 5 + qi_]),
                                 waits=[t_sum], inc="v")
            yield
            r2, mr2, tr2, tmr2 = emit_stats(sq2_tok, sq2_tok, B2)
            sq3_tok = None
            for qi in range(4):
                a_sl = attn_tm[:, qi * 1024:(qi + 1) * 1024]
                t_n2 = rec.op("scalar", lambda s2, qi_=qi, a=a_sl:
                              s2.activation(ot_bf[:], a, ACT.Identity,
                                            bias=mr2[:, qi_:qi_ + 1],
                                            scale=r2[:, qi_:qi_ + 1]),
                              waits=[tr2, tmr2, st_["ot_free"]], inc="s")
                t_add = rec.op("vector", lambda v, qi_=qi: v.tensor_add(
                    x2t[qi_][:], x2t[qi_][:], ot_bf[:]),
                    waits=[t_n2, x_toks[qi]], inc="v")
                st_["ot_free"] = t_add
                t_sum3 = rec.op("vector", lambda v, qi_=qi: v.reduce_sum(
                    stat[:, B3 + qi_:B3 + qi_ + 1], x2t[qi_][:], axis=AX.X),
                    waits=[t_add], inc="v")
                sq3_tok = rec.op("vector", lambda v, qi_=qi:
                                 v.scalar_tensor_tensor(
                                     out=scr_bf[:], in0=x2t[qi_][:],
                                     scalar=1.0, in1=x2t[qi_][:],
                                     op0=ALU.mult, op1=ALU.mult,
                                     accum_out=stat[:, B3 + 4 + qi_:B3 + 5 + qi_]),
                                 waits=[t_sum3], inc="v")
                yield
            r3, mr3, tr3, tmr3 = emit_stats(sq3_tok, sq3_tok, B3)
            t_n3 = None
            for qi in range(4):
                t_n3 = rec.op("scalar", lambda s2, qi_=qi:
                              s2.activation(ln3_tm[:], x2t[qi_][:],
                                            ACT.Identity,
                                            bias=mr3[:, qi_:qi_ + 1],
                                            scale=r3[:, qi_:qi_ + 1]),
                              waits=[tr3, tmr3, st_["ln3_tm_free"]],
                              inc="s")
                t_tp = transpose_group(
                    lambda c: ln3_tm[:, c * 128:(c + 1) * 128], 8,
                    [t_n3, st_["ln3_fm_free"]])
                st_["ln3_tm_free"] = t_tp
                ln3_ev = rec.op("vector", lambda v, qi_=qi: v.tensor_copy(
                    ln3_fm[:].rearrange("p (c x) -> p c x", x=512)
                    [:, :, qi_ * 128:(qi_ + 1) * 128],
                    pT[:].rearrange("p (c x) -> p c x", x=128)),
                    waits=[t_tp], inc="v")
                st_["pt_free"] = ln3_ev
                yield
            ln3_ready = ln3_ev
            st_["attn_tm_free"] = t_n3
            if dbg and g == 0:
                tapd("d_x2", x2t[0][:], st_["ot_free"])
                tapd("d_ln3", ln3_fm[:], ln3_ready)

            p_ready = None
            t_u1 = None
            for mh in range(2):
                par = mh % 2
                bank = pool[mh]
                t_wu1 = rec.dma("sync", "qw", wp_st[par][:],
                                wscr[:, OFF_U1 + mh * 1024:
                                     OFF_U1 + (mh + 1) * 1024],
                                waits=[st_["wp_free"][par], wtok_for(OFF_U1 + mh * 1024, 1024)])

                def f_u1(te, p=par, bk=bank):
                    last = None
                    for c in range(8):
                        last = mm(te, bk[:],
                                  wp_st[p][:, c * 128:(c + 1) * 128],
                                  ln3_fm[:, c * 512:(c + 1) * 512],
                                  c == 0, c == 7)
                    return last
                t_u1 = rec.op("tensor", f_u1,
                              waits=[t_wu1, ln3_ready, st_["pool_free"][mh]],
                              inc="t")
                st_["wp_free"][par] = t_u1
                p_ready = rec.op("vector",
                                 lambda v, mh_=mh, bk=bank: v.tensor_copy(
                                     P_bf[:, mh_ * 512:(mh_ + 1) * 512],
                                     bk[:]),
                                 waits=[t_u1], inc="v")
                st_["pool_free"][mh] = p_ready
                yield

            t_u2 = None
            for fc in range(NFC):
                par = fc % 2
                t_wv = rec.dma("sync", "qw", vu_st[par][:],
                               wscr[:, OFF_V1 + fc * 256:
                                    OFF_V1 + (fc + 1) * 256],
                               waits=[st_["vu_free"][par], wtok_for(OFF_V1 + fc * 256, 256)])
                t_wu = rec.dma("sync", "qw", u2_st[par][:],
                               wscr[:, OFF_U2 + fc * 256:
                                    OFF_U2 + (fc + 1) * 256],
                               waits=[st_["u2_free"][par], wtok_for(OFF_U2 + fc * 256, 256)])

                def f_v1(te, p=par):
                    mm(te, pC[:], vu_st[p][:, 0:128], P_bf[:, 0:512],
                       True, False)
                    return mm(te, pC[:], vu_st[p][:, 128:256],
                              P_bf[:, 512:1024], False, True)
                t_v1 = rec.op("tensor", f_v1,
                              waits=[t_wv, p_ready, st_["pool_free"][2]],
                              inc="t")
                st_["vu_free"][par] = t_v1
                t_g = rec.op("scalar", lambda s2, p=par, fc_=fc:
                             s2.activation(h_bf[p][:], pC[:],
                                           ACT.Gelu_apprx_tanh,
                                           bias=b1c[:, fc_:fc_ + 1]),
                             waits=[t_v1, st_["h_free"][par], t_bias],
                             inc="s")
                st_["pool_free"][2] = t_g

                def f_u2(te, p=par, fc_=fc):
                    mm(te, pA[:], u2_st[p][:, 0:128], h_bf[p][:],
                       fc_ == 0, fc_ == NFC - 1)
                    return mm(te, pB[:], u2_st[p][:, 128:256], h_bf[p][:],
                              fc_ == 0, fc_ == NFC - 1)
                w_u2 = [t_wu, t_g]
                if fc == 0:
                    w_u2 += [st_["pool_free"][0], st_["pool_free"][1]]
                t_u2 = rec.op("tensor", f_u2, waits=w_u2, inc="t")
                st_["u2_free"][par] = t_u2
                st_["h_free"][par] = t_u2

            def f_q2(v):
                v.tensor_copy(q2_bf[:, 0:512], pA[:])
                return v.tensor_copy(q2_bf[:, 512:1024], pB[:])
            t_q2 = rec.op("vector", f_q2,
                          waits=[t_u2, st_["q2_free"]], inc="v")
            st_["pool_free"][0] = t_q2
            st_["pool_free"][1] = t_q2
            yield
            if dbg and g == 0:
                tapd("d_P", P_bf[:], p_ready)
                tapd("d_q2", q2_bf[:], t_q2)

            t_v2 = None
            for qi in range(4):
                tch = s * 4 + qi
                b4_ = 96 + (qi % 2) * 16

                def f_v2(te, qi_=qi):
                    last = None
                    for seg in range(2):
                        bk = pool[seg]
                        for ch in range(2):
                            last = mm(te, bk[:],
                                      q2_bf[:, ch * 512 + qi_ * 128:
                                            ch * 512 + (qi_ + 1) * 128],
                                      v2_bf[:, ch * 1024 + seg * 512:
                                            ch * 1024 + (seg + 1) * 512],
                                      ch == 0, ch == 1)
                    return last
                t_v2 = rec.op("tensor", f_v2,
                              waits=[t_q2, st_["pool_free"][0],
                                     st_["pool_free"][1], cv_state["cv_tok"]], inc="t")

                def f_b2(v, b=b4_):
                    v.scalar_tensor_tensor(
                        out=o1[:, 0:512], in0=pA[:], scalar=1.0,
                        in1=b2r[:, 0:512], op0=ALU.mult, op1=ALU.add,
                        accum_out=stat[:, b + 6:b + 7])
                    return v.scalar_tensor_tensor(
                        out=o1[:, 512:1024], in0=pB[:], scalar=1.0,
                        in1=b2r[:, 512:1024], op0=ALU.mult, op1=ALU.add,
                        accum_out=stat[:, b + 7:b + 8])
                t_b2o = rec.op("vector", f_b2,
                               waits=[t_v2, t_bias, st_["o1_free"]],
                               inc="v")
                st_["pool_free"][0] = t_b2o
                st_["pool_free"][1] = t_b2o
                t_sum4 = rec.op("vector", lambda v, b=b4_: v.tensor_add(
                    stat[:, b:b + 1], stat[:, b + 6:b + 7],
                    stat[:, b + 7:b + 8]), waits=[t_b2o], inc="v")
                t_sq4 = rec.op("vector", lambda v, b=b4_:
                               v.scalar_tensor_tensor(
                                   out=scr_bf[:], in0=o1[:], scalar=1.0,
                                   in1=o1[:], op0=ALU.mult, op1=ALU.mult,
                                   accum_out=stat[:, b + 1:b + 2]),
                               waits=[t_b2o], inc="v")
                r4v, mr4v, tr4, tmr4 = emit_stats(t_sum4, t_sq4, b4_, nb=1)
                t_aff = rec.op("vector", lambda v, r_=r4v, m_=mr4v: v.tensor_scalar(
                    out=o1[:], in0=o1[:], scalar1=r_, scalar2=m_,
                    op0=ALU.mult, op1=ALU.add),
                    waits=[tr4, tmr4, t_sq4], inc="v")
                t_fin = rec.op("vector", lambda v, qi_=qi: v.tensor_add(
                    x2t[qi_][:], x2t[qi_][:], o1[:]), waits=[t_aff], inc="v")
                t_st = rec.dma("gpsimd", "qx", out_view[seq, tch, :, :],
                               x2t[qi][:], waits=[t_fin])
                st_["x2_free"][qi] = t_st
                yield
            st_["ln3_fm_free"] = t_u1

        def mk(kind, g):
            sq, sp = g // NSUP, g % NSUP
            if kind == "conv":
                return gen_conv()
            if kind == "p2":
                return gen_p2(sq, sp)
            if kind == "att":
                return gen_att(sq, sp)
            return gen_f(sq, sp)

        slots = [
            [("conv", 0), ("p2", 0)],
            [("att", 0), ("p2", 1)],
            [("att", 1), ("p2", 2), ("f", 0)],
            [("att", 2), ("p2", 3), ("f", 1)],
            [("att", 3), ("f", 2)],
            [("p2", 4), ("f", 3)],
            [("att", 4), ("p2", 5)],
            [("att", 5), ("p2", 6), ("f", 4)],
            [("att", 6), ("p2", 7), ("f", 5)],
            [("att", 7), ("f", 6)],
            [("f", 7)],
        ]
        if dbg:
            slots = ([[("p2", g)], [("att", g)], [("f", g)]]
                     for g in range(SEQS * NSUP))
            slots = [[("conv", 0)]] + [x for g in slots for x in g]
        for slot in slots:
            alive = [(k, g, mk(k, g)) for (k, g) in slot]
            while alive:
                for item in list(alive):
                    k, g, gn = item
                    try:
                        next(gn)
                    except StopIteration:
                        alive.remove(item)
                        if k == "att":
                            done_att[g] = True

        rec.op("gpsimd", lambda g: g.nop(), waits=[("qx", rec.cnt["qx"])])

        sems = {}
        for name in ("qw", "qx", "qs", "g", "v", "s", "t"):
            sems[name] = es.enter_context(nc.semaphore(name))

        block = es.enter_context(nc.Block())

        def replay(engname):
            def runner(eng):
                last_wait = {}
                for waits, inc_sem, amt, f in rec.q[engname]:
                    for (sn, val) in waits:
                        if last_wait.get(sn, -1) >= val:
                            continue
                        eng.wait_ge(sems[sn], val)
                        last_wait[sn] = val
                    ins = f(eng)
                    if inc_sem is not None and ins is not None:
                        ins.then_inc(sems[inc_sem], amt)
            return runner

        block.sync(replay("sync"))
        block.gpsimd(replay("gpsimd"))
        block.vector(replay("vector"))
        block.scalar(replay("scalar"))
        block.tensor(replay("tensor"))

    return nc


def make_images(inputs):
    f = np.float32
    W = np.asarray(inputs["L0_attn_w_attn"], f)
    b_attn = np.asarray(inputs["L0_attn_b_attn"], f)
    la = np.asarray(inputs["L0_attn_la_attn"], f)
    lb = np.asarray(inputs["L0_attn_lb_attn"], f)
    Wp = np.asarray(inputs["L0_attn_w_proj"], f)
    b_proj = np.asarray(inputs["L0_attn_b_proj"], f)
    lap = np.asarray(inputs["L0_attn_la_proj"], f)
    lbp = np.asarray(inputs["L0_attn_lb_proj"], f)
    U1 = np.asarray(inputs["L0_ffn_U1"], f)
    V1 = np.asarray(inputs["L0_ffn_V1"], f)
    b1 = np.asarray(inputs["L0_ffn_b1"], f)
    U2 = np.asarray(inputs["L0_ffn_U2"], f)
    V2 = np.asarray(inputs["L0_ffn_V2"], f)
    b2 = np.asarray(inputs["L0_ffn_b2"], f)

    qkv = W.T.reshape(8, 128, 3072).transpose(1, 0, 2)
    m_slices = [np.ascontiguousarray(qkv[:, :, m * 128:(m + 1) * 128]
                                     .reshape(128, 1024)) for m in range(24)]
    la_img = np.ascontiguousarray(la.T.reshape(8, 128, 32).transpose(1, 0, 2)
                                  .reshape(128, 256))
    wpT = Wp.T.reshape(8, 128, 1024).transpose(1, 0, 2)
    wp_slices = [np.ascontiguousarray(wpT[:, :, m * 128:(m + 1) * 128]
                                      .reshape(128, 1024)) for m in range(8)]
    v1r = V1.reshape(2, 128, 4096).transpose(1, 0, 2)
    v1_slices = [np.ascontiguousarray(v1r[:, :, fc * 128:(fc + 1) * 128]
                                      .reshape(128, 256)) for fc in range(32)]
    u2r = U2.reshape(32, 128, 256).transpose(1, 0, 2)
    u2_slices = [np.ascontiguousarray(u2r[:, fc, :].reshape(128, 256))
                 for fc in range(32)]
    u1r = U1.reshape(8, 128, 256).transpose(1, 0, 2)
    u1_slices = [np.ascontiguousarray(u1r[:, :, mh * 128:(mh + 1) * 128]
                                      .reshape(128, 1024)) for mh in range(2)]
    wbig = np.ascontiguousarray(
        np.concatenate(m_slices + [la_img] + wp_slices + v1_slices + u2_slices
                       + u1_slices, axis=1))
    assert wbig.shape == (128, WCOLS), wbig.shape

    v2_img = V2.reshape(2, 128, 1024).transpose(1, 0, 2).reshape(128, 2048)
    lap_img = lap.T.reshape(8, 128, 32).transpose(1, 0, 2).reshape(128, 256)
    wsmall = np.ascontiguousarray(
        np.concatenate([v2_img, lap_img], axis=1))

    wlb = np.zeros((32, 4096), f)
    wlb[:, 0:3072] = lb.T
    wlb[:, 3072:4096] = lbp.T

    return dict(
        wbig=wbig, wsmall=wsmall, wlb=wlb,
        bqk=np.ascontiguousarray(b_attn.reshape(24, 128).T),
        b1c=np.ascontiguousarray(b1.reshape(32, 128).T),
        bp=np.ascontiguousarray(b_proj.reshape(8, 128).T),
        b2r=np.ascontiguousarray(np.broadcast_to(b2[None, :], (128, 1024))),
    )


def kernel(**inputs):
    imgs = make_images(inputs)
    x = np.asarray(inputs["L0_attn_x"], np.float32)
    nc = build_nc()
    in_maps = []
    for c in range(NCORE):
        m = dict(imgs)
        m["x"] = np.ascontiguousarray(x[SEQS * c:SEQS * (c + 1)])
        in_maps.append(m)
    res = run_bass_kernel_spmd(nc, in_maps, core_ids=list(range(NCORE)))
    out = np.empty_like(x)
    for c in range(NCORE):
        out[SEQS * c:SEQS * (c + 1)] = res.results[c]["out"]
    return out


# version 27
# speedup vs baseline: 1.2791x; 1.0059x over previous
import numpy as np
from contextlib import ExitStack

import concourse.bass as bass
import concourse.mybir as mybir
from concourse.bass_utils import run_bass_kernel_spmd

F32 = mybir.dt.float32
BF16 = mybir.dt.bfloat16
AX = mybir.AxisListType
ALU = mybir.AluOpType
ACT = mybir.ActivationFunctionType

NCORE = 8
SEQS = 2
T = 2048
C = 1024
NSUP = 4
NFC = 32

OFF_QKV = 0
OFF_LA = 24 * 1024
OFF_WP = OFF_LA + 256
OFF_V1 = OFF_WP + 8192
OFF_U2 = OFF_V1 + 8192
OFF_U1 = OFF_U2 + 8192
WCOLS = OFF_U1 + 2048

ENGS = ("sync", "gpsimd", "vector", "scalar", "tensor")


class Rec:

    def __init__(self):
        self.q = {e: [] for e in ENGS}
        self.cnt = {}
        self.own = {"vector": "v", "scalar": "s", "tensor": "t", "gpsimd": "g",
                    "sync": None}

    def sem(self, name):
        self.cnt[name] = 0

    def op(self, eng, f, waits=(), inc=None, amt=1):
        ws = tuple(w for w in waits if w is not None)
        tok = None
        if inc is not None:
            self.cnt[inc] += amt
            tok = (inc, self.cnt[inc])
        self.q[eng].append((ws, inc, amt, f))
        return tok

    def dma(self, eng, qname, out_ap, in_ap, waits=()):
        ws = tuple(w for w in waits if w is not None)
        self.cnt[qname] += 16
        tok = (qname, self.cnt[qname])

        def f(e, _o=out_ap, _i=in_ap):
            return e.dma_start(out=_o, in_=_i)

        self.q[eng].append((ws, qname, 16, f))
        return tok


def build_nc(dbg=None):
    nc = bass.Bass("TRN2", target_bir_lowering=False)

    x_h = nc.dram_tensor("x", [SEQS, T, C], F32, kind="ExternalInput").ap()
    wbig = nc.dram_tensor("wbig", [128, WCOLS], F32, kind="ExternalInput").ap()
    wsmall = nc.dram_tensor("wsmall", [128, 2304], F32, kind="ExternalInput").ap()
    wlb = nc.dram_tensor("wlb", [32, 4096], F32, kind="ExternalInput").ap()
    bqk_h = nc.dram_tensor("bqk", [128, 24], F32, kind="ExternalInput").ap()
    b1c_h = nc.dram_tensor("b1c", [128, 32], F32, kind="ExternalInput").ap()
    bp_h = nc.dram_tensor("bp", [128, 8], F32, kind="ExternalInput").ap()
    b2r_h = nc.dram_tensor("b2r", [128, 1024], F32, kind="ExternalInput").ap()
    out_h = nc.dram_tensor("out", [SEQS, T, C], F32, kind="ExternalOutput").ap()
    dbg_t = {}
    if dbg:
        for name, shape in (("d_ln1", [128, 4096]), ("d_q", [128, 4096]),
                            ("d_k", [128, 8 * T]), ("d_v", [128, 16 * 1040]),
                            ("d_plq", [32, 512]), ("d_y", [128, 4096]),
                            ("d_yacc", [128, 520]),
                            ("d_attn", [128, 4096]), ("d_x2", [128, 1024]),
                            ("d_ln3", [128, 4096]), ("d_P", [128, 1024]),
                            ("d_q2", [128, 1024]), ("d_yfm", [128, 4096])):
            if name in dbg.get("taps", ()):
                dbg_t[name] = nc.dram_tensor(name, shape, F32, kind="ExternalOutput").ap()
    wscr = nc.dram_tensor("wscr", [128, WCOLS], BF16, kind="Internal").ap()

    x_view = x_h.rearrange("s (n p) c -> s n p c", p=128)
    out_view = out_h.rearrange("s (n p) c -> s n p c", p=128)

    es = ExitStack()
    SB = lambda n, s, d: es.enter_context(nc.sbuf_tensor(n, s, d))
    PS = lambda n, s, d: es.enter_context(nc.psum_tensor(n, s, d))

    with es:
        iden = SB("iden", [128, 128], BF16)
        tri = SB("tri", [128, 128], BF16)
        cm20 = SB("cm20", [128, 1], F32)
        ceps = SB("ceps", [128, 1], F32)
        lbq_bf = SB("lbq_bf", [32, 3072], BF16)
        lbp_bf = SB("lbp_bf", [32, 1024], BF16)
        lap_bf = SB("lap_bf", [128, 256], BF16)
        v2_bf = SB("v2_bf", [128, 2048], BF16)
        bqk = SB("bqk_s", [128, 24], F32)
        b1c = SB("b1c_s", [128, 32], F32)
        bp = SB("bp_s", [128, 8], F32)
        b2r = SB("b2r_s", [128, 1024], BF16)
        k_fm = SB("k_fm", [128, 8 * T], BF16)
        v_tm = SB("v_tm", [128, 16 * 1040], BF16)
        q_sup = SB("q_sup", [128, 4096], BF16)
        y_sup2 = [SB(f"y_sup{i}", [128, 4096], BF16) for i in range(2)]
        e_bf = [SB(f"e_bf{i}", [128, 1024], BF16) for i in range(4)]
        y_acc = SB("y_acc", [128, 520], F32)
        r8 = SB("r8", [128, 8], F32)
        ln1_fm = SB("ln1_fm", [128, 4096], BF16)
        ln1_tm = SB("ln1_tm", [128, 1024], BF16)
        ln3_tm = SB("ln3_tm", [128, 1024], BF16)
        ln3_fm = SB("ln3_fm", [128, 4096], BF16)
        y_fm = SB("y_fm", [128, 4096], BF16)
        attn_tm = SB("attn_tm", [128, 4096], BF16)
        vf_bf = [SB(f"vf_bf{i}", [128, 512], BF16) for i in range(2)]
        po_bf = [SB(f"po_bf{i}", [128, 512], BF16) for i in range(2)]
        P_bf = SB("P_bf", [128, 1024], BF16)
        h_bf = [SB(f"h_bf{i}", [128, 512], BF16) for i in range(2)]
        q2_bf = SB("q2_bf", [128, 1024], BF16)
        plq_bf = SB("plq_bf", [32, 512], BF16)
        plp_bf = SB("plp_bf", [32, 512], BF16)
        ot_bf = SB("ot_bf", [128, 1024], BF16)
        scr_bf = SB("scr_bf", [128, 1024], BF16)
        o1 = SB("o1", [128, 1024], F32)
        stat = SB("stat", [128, 128], F32)
        xa = [SB(f"xa{i}", [128, 1024], F32) for i in range(2)]
        x2t = [SB(f"x2t{i}", [128, 1024], F32) for i in range(4)]
        wq_st = [SB(f"wq_st{i}", [128, 1024], BF16) for i in range(3)]
        wp_st = [SB(f"wp_st{i}", [128, 1024], BF16) for i in range(2)]
        vu_st = [SB(f"vu_st{i}", [128, 512], BF16) for i in range(2)]
        u2_st = [SB(f"u2_st{i}", [128, 256], BF16) for i in range(2)]

        pA = PS("pA", [128, 512], F32)
        pB = PS("pB", [128, 512], F32)
        pC = PS("pC", [128, 512], F32)
        pAtt = PS("pAtt", [128, 1024], F32)
        pY0 = PS("pY0", [128, 512], F32)
        pY1 = PS("pY1", [128, 512], F32)
        pT = PS("pT", [128, 1024], BF16)
        pool = [pA, pB, pC]

        rec = Rec()
        for s_ in ("qw", "qx", "qs", "g", "v", "s", "t"):
            rec.sem(s_)

        st_ = {
            "pt_free": None, "pool_free": [None, None, None],
            "xa_free": [None, None], "wq_free": [None, None, None],
            "wp_free": [None, None], "ysup_free": [None, None],
            "vu_free": [None, None], "u2_free": [None, None],
            "vf_free": [None, None], "po_free": [None, None],
            "h_free": [None, None], "x2_free": [None] * 4,
            "ebf_free": [None] * 4, "pY_free": None,
            "attexp_last": None, "ln1_tm_free": None, "ln3_tm_free": None,
            "ln1_fm_free": None, "ln3_fm_free": None, "y_fm_free": None,
            "attn_tm_free": None, "qsup_free": None,
            "kv_free": None, "ot_free": None,
            "o1_free": None, "q2_free": None,
        }

        def gp_consts(g):
            g.memset(iden[:], 1.0)
            g.affine_select(out=iden[:], in_=iden[:], compare_op=ALU.is_equal,
                            fill=0.0, base=0, pattern=[[-1, 128]],
                            channel_multiplier=1)
            g.memset(tri[:], 0.0)
            g.affine_select(out=tri[:], in_=tri[:], compare_op=ALU.is_gt,
                            fill=1.0, base=0, pattern=[[-1, 128]],
                            channel_multiplier=1)
            g.memset(cm20[:], -20.0)
            g.memset(ceps[:], 1e-5)
            return g.memset(
                v_tm[:].rearrange("p (k h d) -> p k h d", h=16, d=65)[:, :, :, 64:65],
                1.0)

        t_consts = rec.op("gpsimd", gp_consts, inc="g")

        t_bias = None
        for dst, src in ((bqk, bqk_h), (b1c, b1c_h), (bp, bp_h)):
            t_bias = rec.dma("sync", "qw", dst[:], src)
        t_bias2 = rec.dma("gpsimd", "qx", b2r[:], b2r_h)

        wtok = {}
        cv_state = {"cv_tok": None}

        CONV_CHUNKS = (
            [(OFF_LA, 256, "la")]
            + [(m * 1024, 1024, f"qkv{m}") for m in range(16, 24)]
            + [(m * 1024, 1024, f"qkv{m}") for m in range(8, 16)]
            + [(m * 1024, 1024, f"qkv{m}") for m in range(8)]
            + [(OFF_WP + m * 1024, 1024, f"wp{m}") for m in range(8)]
            + [(OFF_U1 + m * 1024, 1024, f"u1{m}") for m in range(2)]
            + [x for c in range(8) for x in
               ((OFF_V1 + c * 1024, 1024, f"v1{c}"),
                (OFF_U2 + c * 1024, 1024, f"u2{c}"))]
        )

        def wtok_for(off, width):
            for (o, w, key) in CONV_CHUNKS:
                if o <= off and off + width <= o + w:
                    assert key in wtok, (off, width, key)
                    return wtok[key]
            raise AssertionError((off, width))

        def gen_conv():
            smalls = [
                (v2_bf[:, 0:1024], wsmall[:, 0:1024], 128),
                (v2_bf[:, 1024:2048], wsmall[:, 1024:2048], 128),
                (lap_bf[:, 0:256], wsmall[:, 2048:2304], 128),
                (lbq_bf[:, 0:1024], wlb[0:32, 0:1024], 32),
                (lbq_bf[:, 1024:2048], wlb[0:32, 1024:2048], 32),
                (lbq_bf[:, 2048:3072], wlb[0:32, 2048:3072], 32),
                (lbp_bf[:, 0:1024], wlb[0:32, 3072:4096], 32),
            ]
            fst = [x2t[0], x2t[1]]
            fguard = [None, None]
            for i, (dst, srcap, np_) in enumerate(smalls):
                par = i % 2
                nf = srcap.shape[-1]
                t_ld = rec.dma("sync", "qw", fst[par][0:np_, 0:nf], srcap,
                               waits=[fguard[par]])
                tok = rec.op(
                    "vector",
                    lambda v, d=dst, p=par, n0=np_, n1=nf: v.tensor_copy(
                        d, fst[p][0:n0, 0:n1]),
                    waits=[t_ld], inc="v")
                fguard[par] = tok
                cv_state["cv_tok"] = tok
                if i % 2 == 1:
                    yield
            fbig = [o1, x2t[2]]
            bbig = [q2_bf, ot_bf]
            fg = [None, None]
            bg = [None, None]
            for i, (off, w, key) in enumerate(CONV_CHUNKS):
                par = i % 2
                t_ld = rec.dma("sync", "qw", fbig[par][:, 0:w],
                               wbig[:, off:off + w], waits=[fg[par]])
                t_cv = rec.op(
                    "vector",
                    lambda v, p=par, w_=w: v.tensor_copy(bbig[p][:, 0:w_],
                                                         fbig[p][:, 0:w_]),
                    waits=[t_ld, bg[par]], inc="v")
                fg[par] = t_cv
                t_st = rec.dma("scalar", "qs", wscr[:, off:off + w],
                               bbig[par][:, 0:w], waits=[t_cv])
                bg[par] = t_st
                wtok[key] = t_st
                if i % 2 == 1:
                    yield
            st_["x2_free"] = [fguard[0], fguard[1], fg[1], None]
            st_["o1_free"] = fg[0]
            st_["q2_free"] = bg[0]
            st_["ot_free"] = bg[1]

        def mm(te, out_ap, lh, rh, start, stop, tp=None):
            return te.matmul(out_ap, lh, rh, start=start, stop=stop,
                             tile_position=tp)

        def transpose_group(src_fn, nblk, waits):
            def f(te):
                last = None
                for c in range(nblk):
                    last = te.transpose(pT[:, c * 128:(c + 1) * 128],
                                        src_fn(c), iden[:])
                return last
            return rec.op("tensor", f,
                          waits=list(waits) + [st_["pt_free"], t_consts],
                          inc="t")

        def emit_stats(t_sum, t_sq, base, nb=4):
            c = lambda j: stat[:, base + j * nb: base + (j + 1) * nb]
            t1 = rec.op("scalar", lambda s2: s2.activation(
                c(2), c(0), ACT.Identity, scale=1.0 / 1024), waits=[t_sum], inc="s")
            t2 = rec.op("vector", lambda v: v.scalar_tensor_tensor(
                out=c(3), in0=c(2), scalar=-1.0, in1=c(2),
                op0=ALU.mult, op1=ALU.mult), waits=[t1], inc="v")
            t2b = rec.op("vector", lambda v: v.tensor_scalar(
                out=c(6), in0=c(1), scalar1=1.0 / 1024, scalar2=ceps[:],
                op0=ALU.mult, op1=ALU.add), waits=[t_sq, t_consts], inc="v")
            t2c = rec.op("vector", lambda v: v.tensor_add(c(6), c(6), c(3)),
                         waits=[t2, t2b], inc="v")
            t3 = rec.op("scalar", lambda s2: s2.activation(
                c(7), c(6), ACT.Ln), waits=[t2c], inc="s")
            t4 = rec.op("scalar", lambda s2: s2.activation(
                c(4), c(7), ACT.Exp, scale=-0.5), waits=[t3], inc="s")
            t5 = rec.op("vector", lambda v: v.scalar_tensor_tensor(
                out=c(5), in0=c(2), scalar=-1.0, in1=c(4),
                op0=ALU.mult, op1=ALU.mult), waits=[t4], inc="v")
            return c(4), c(5), t4, t5

        def tapd(name, buf_ap, tok):
            if not (dbg and name in dbg_t):
                return
            w = buf_ap.shape[-1]
            npart = buf_ap.shape[0]
            done = 0
            while done < w:
                seg = min(1024, w - done)
                t_cv = rec.op("vector",
                              lambda v, b=buf_ap, d0=done, sg=seg, np_=npart:
                              v.tensor_copy(o1[0:np_, 0:sg], b[0:np_, d0:d0 + sg]),
                              waits=[tok, st_.get("dbg_free")], inc="v")
                st_["dbg_free"] = rec.dma("gpsimd", "qx",
                                          dbg_t[name][0:npart, done:done + seg],
                                          o1[0:npart, 0:seg], waits=[t_cv])
                done += seg

        seqd = {sq: {"k": [], "v": []} for sq in range(SEQS)}
        sup_tok = {}
        done_att = {}

        def gen_p2(seq, s):
            g = seq * NSUP + s
            ln1_ev = None
            for tl in range(4):
                tch = s * 4 + tl
                par = tl % 2
                base = par * 8
                t_x = rec.dma("gpsimd", "qx", xa[par][:],
                              x_view[seq, tch, :, :],
                              waits=[st_["xa_free"][par]])
                t_sum = rec.op("vector", lambda v, p=par, b=base: v.reduce_sum(
                    stat[:, b:b + 1], xa[p][:], axis=AX.X),
                    waits=[t_x], inc="v")
                t_sq = rec.op("vector", lambda v, p=par, b=base: v.scalar_tensor_tensor(
                    out=scr_bf[:], in0=xa[p][:], scalar=1.0, in1=xa[p][:],
                    op0=ALU.mult, op1=ALU.mult,
                    accum_out=stat[:, b + 1:b + 2]),
                    waits=[t_sum], inc="v")
                r1, mr1, tr, tmr = emit_stats(t_sum, t_sq, base, nb=1)
                t_n = rec.op("scalar", lambda s2, p=par, r_=r1, m_=mr1:
                             s2.activation(ln1_tm[:], xa[p][:], ACT.Identity,
                                           bias=m_, scale=r_),
                             waits=[tr, tmr, st_["ln1_tm_free"]], inc="s")
                st_["xa_free"][par] = t_n
                t_tp = transpose_group(
                    lambda c: ln1_tm[:, c * 128:(c + 1) * 128], 8,
                    [t_n, st_["ln1_fm_free"]])
                st_["ln1_tm_free"] = t_tp
                ln1_ev = rec.op("vector", lambda v, tl_=tl: v.tensor_copy(
                    ln1_fm[:].rearrange("p (c x) -> p c x", x=512)
                    [:, :, tl_ * 128:(tl_ + 1) * 128],
                    pT[:].rearrange("p (c x) -> p c x", x=128)),
                    waits=[t_tp], inc="v")
                st_["pt_free"] = ln1_ev
                yield
            ln1_ready = ln1_ev
            if dbg and g == 0:
                tapd("d_ln1", ln1_fm[:], ln1_ready)

            t_la = rec.dma("sync", "qw", wq_st[0][:, 0:256],
                           wscr[:, OFF_LA:OFF_LA + 256],
                           waits=[st_["wq_free"][0], wtok_for(OFF_LA, 256)])

            def f_la(te):
                last = None
                for c in range(8):
                    last = mm(te, pC[0:32, :],
                              wq_st[0][:, c * 32:(c + 1) * 32],
                              ln1_fm[:, c * 512:(c + 1) * 512],
                              c == 0, c == 7)
                return last
            t_lam = rec.op("tensor", f_la,
                           waits=[t_la, ln1_ready, st_["pool_free"][2],
                                  cv_state["cv_tok"]], inc="t")
            t_plq = rec.op("scalar", lambda s2: s2.activation(
                plq_bf[:, :], pC[0:32, :], ACT.Identity, scale=0.5),
                waits=[t_lam], inc="s")
            st_["pool_free"][2] = t_plq
            st_["wq_free"][0] = t_lam
            yield

            q_ready = None
            t_mm = None
            morder = list(range(16, 24)) + list(range(8, 16)) + list(range(8))
            for mi, m in enumerate(morder):
                if m == 0 and g > 0:
                    while not done_att.get(g - 1, False):
                        yield
                par = mi % 3
                bi = mi % 2
                bank = pool[bi]
                t_w = rec.dma("sync", "qw", wq_st[par][:],
                              wscr[:, OFF_QKV + m * 1024:
                                   OFF_QKV + (m + 1) * 1024],
                              waits=[st_["wq_free"][par], wtok_for(OFF_QKV + m * 1024, 1024)])

                def f_m(te, p=par, bk=bank, m_=m):
                    last = None
                    for c in range(8):
                        last = mm(te, bk[:],
                                  wq_st[p][:, c * 128:(c + 1) * 128],
                                  ln1_fm[:, c * 512:(c + 1) * 512],
                                  c == 0, False)
                    return mm(te, bk[:],
                              lbq_bf[0:32, m_ * 128:(m_ + 1) * 128],
                              plq_bf[0:32, :], False, True)
                t_mm = rec.op("tensor", f_m,
                              waits=[t_w, ln1_ready, t_plq,
                                     st_["pool_free"][bi], cv_state["cv_tok"]],
                              inc="t")
                st_["wq_free"][par] = t_mm
                if m < 8:
                    t_ev = rec.op("scalar",
                                  lambda s2, bk=bank, m_=m: s2.activation(
                                      q_sup[:, m_ * 512:(m_ + 1) * 512],
                                      bk[:], ACT.Identity,
                                      bias=bqk[:, m_:m_ + 1]),
                                  waits=[t_mm, t_bias, st_["qsup_free"]],
                                  inc="s")
                    st_["pool_free"][bi] = t_ev
                    if m == 7:
                        q_ready = t_ev
                elif m < 16:
                    mk = m - 8
                    t_ev = rec.op("scalar",
                                  lambda s2, bk=bank, mk_=mk, m_=m, s_=s:
                                  s2.activation(
                                      k_fm[:, mk_ * 2048 + s_ * 512:
                                           mk_ * 2048 + (s_ + 1) * 512],
                                      bk[:], ACT.Identity,
                                      bias=bqk[:, m_:m_ + 1]),
                                  waits=[t_mm, t_bias, st_["kv_free"]],
                                  inc="s")
                    st_["pool_free"][bi] = t_ev
                    if m == 15:
                        seqd[seq]["k"].append(t_ev)
                else:
                    mv = m - 16
                    vp = mi % 2
                    t_ev = rec.op("scalar",
                                  lambda s2, bk=bank, m_=m, vp_=vp:
                                  s2.activation(vf_bf[vp_][:], bk[:],
                                                ACT.Identity,
                                                bias=bqk[:, m_:m_ + 1]),
                                  waits=[t_mm, t_bias,
                                         st_["vf_free"][vp]], inc="s")
                    st_["pool_free"][bi] = t_ev
                    t_tp = transpose_group(
                        lambda c, vp_=vp: vf_bf[vp_][:, c * 128:(c + 1) * 128],
                        4, [t_ev, st_["kv_free"]])
                    st_["vf_free"][vp] = t_tp
                    t_vev = rec.op("vector", lambda v, mv_=mv, s_=s: v.tensor_copy(
                        v_tm[:].rearrange("p (k h d) -> p k h d", h=16, d=65)
                        [:, s_ * 4:(s_ + 1) * 4, 2 * mv_:2 * mv_ + 2, 0:64],
                        pT[:, 0:512].rearrange("p (k hd) -> p k hd", hd=128)
                        .rearrange("p k (h d) -> p k h d", d=64)),
                        waits=[t_tp], inc="v")
                    st_["pt_free"] = t_vev
                    if m == 23:
                        seqd[seq]["v"].append(t_vev)
                yield
            st_["ln1_fm_free"] = t_mm
            sup_tok[g] = {"q": q_ready}
            if dbg and g == 0:
                tapd("d_q", q_sup[:], q_ready)
                tapd("d_k", k_fm[:], q_ready)
                tapd("d_v", v_tm[:], q_ready)
                tapd("d_plq", plq_bf[:], t_plq)

        def gen_att(seq, s):
            g = seq * NSUP + s
            y_sup = y_sup2[g % 2]
            q_ready = sup_tok[g]["q"]
            k_list = seqd[seq]["k"]
            v_list = seqd[seq]["v"]
            att_pv_last = None
            norm_last = None
            for pair in range(8):
                chunk_add = None
                for kc in range(s + 1):
                    exp3 = None
                    mask_tok = None
                    for kk in range(4):
                        ki = kc * 4 + kk
                        q0 = kk * 128 if kc == s else 0

                        def f_qk(te, pr=pair, ki_=ki, q0_=0):
                            mm(te, pAtt[:, q0_:512],
                               k_fm[0:64, pr * 2048 + ki_ * 128:
                                    pr * 2048 + (ki_ + 1) * 128],
                               q_sup[0:64, pr * 512 + q0_:(pr + 1) * 512],
                               True, True, tp=(0, 0))
                            return mm(te, pAtt[:, 512 + q0_:1024],
                                      k_fm[64:128, pr * 2048 + ki_ * 128:
                                           pr * 2048 + (ki_ + 1) * 128],
                                      q_sup[64:128, pr * 512 + q0_:(pr + 1) * 512],
                                      True, True, tp=(64, 0))
                        t_qk = rec.op("tensor", f_qk,
                                      waits=[st_["attexp_last"], q_ready,
                                             k_list[kc]], inc="t")
                        t_exp = rec.op("scalar", lambda s2, kk_=kk, q0_=q0:
                                       s2.activation(
                                           e_bf[kk_][:].rearrange(
                                               "p (h x) -> p h x", x=512)[:, :, q0_:512],
                                           pAtt[:].rearrange(
                                               "p (h x) -> p h x", x=512)[:, :, q0_:512],
                                           ACT.Exp, scale=0.125,
                                           bias=cm20[:]),
                                       waits=[t_qk], inc="s")
                        st_["attexp_last"] = t_exp
                        exp3 = t_exp
                        yield
                    if kc == s:
                        def f_mask(v):
                            last = None
                            for kk in range(4):
                                sl_ap = (e_bf[kk][:]
                                         .rearrange("p (h x) -> p h x", x=512)
                                         [:, :, kk * 128:(kk + 1) * 128])
                                last = v.tensor_tensor(
                                    out=sl_ap, in0=sl_ap,
                                    in1=tri[:].unsqueeze(1)
                                    .broadcast_to([128, 2, 128]),
                                    op=ALU.mult)
                            return last
                        mask_tok = rec.op("vector", f_mask,
                                          waits=[exp3, t_consts], inc="v")

                    first = kc == 0
                    t_pv = None
                    for half in range(2):
                        def f_pv(te, kc_=kc, pr=pair, s_=s, hf=half):
                            last = None
                            for sl in range(hf * 4, hf * 4 + 4):
                                qi_loc, hh = sl // 2, sl % 2
                                kmax = 3 if kc_ < s_ else qi_loc
                                bank = pY0 if hf == 0 else pY1
                                o = (sl % 4) * 128
                                for kk in range(kmax + 1):
                                    ki = kc_ * 4 + kk
                                    last = mm(
                                        te, bank[:, o:o + 65],
                                        e_bf[kk][:, hh * 512 + qi_loc * 128:
                                                 hh * 512 + (qi_loc + 1) * 128],
                                        v_tm[:, ki * 1040 + (2 * pr + hh) * 65:
                                             ki * 1040 + (2 * pr + hh) * 65 + 65],
                                        kk == 0, kk == kmax)
                            return last
                        t_pv = rec.op("tensor", f_pv,
                                      waits=[exp3, mask_tok, v_list[kc],
                                             st_["pY_free"], t_consts], inc="t")

                        def f_acc(v, fr=first, hf=half):
                            d = y_acc[:].rearrange("p (sl c) -> p sl c", c=65)
                            src = (pY0 if hf == 0 else pY1)[:].rearrange(
                                "p (sl c) -> p sl c", c=128)[:, :, 0:65]
                            dd = d[:, hf * 4:hf * 4 + 4, :]
                            if fr:
                                return v.tensor_copy(dd, src)
                            return v.tensor_add(dd, dd, src)
                        chunk_add = rec.op("vector", f_acc, waits=[t_pv],
                                           inc="v")
                        yield
                    att_pv_last = t_pv
                    st_["pY_free"] = chunk_add

                t_rec = rec.op("vector", lambda v: v.reciprocal(
                    r8[:],
                    y_acc[:].rearrange("p (sl c) -> p sl c", c=65)
                    [:, :, 64:65].rearrange("p a b -> p (a b)")),
                    waits=[chunk_add], inc="v")

                def f_norm(v, pr=pair, ys=y_sup):
                    return v.tensor_tensor(
                        out=ys[:].rearrange("p (q rest) -> p q rest",
                                            rest=1024)
                        [:, :, pr * 128:(pr + 1) * 128]
                        .rearrange("p q (h d) -> p q h d", d=64),
                        in0=y_acc[:].rearrange("p (q h c) -> p q h c",
                                               h=2, c=65)[:, :, :, 0:64],
                        in1=r8[:].rearrange("p (a b) -> p a b", b=2)
                        .unsqueeze(3).broadcast_to([128, 4, 2, 64]),
                        op=ALU.mult)
                norm_last = rec.op("vector", f_norm,
                                   waits=[t_rec, st_["ysup_free"][g % 2]],
                                   inc="v")
                yield
            st_["qsup_free"] = att_pv_last
            sup_tok[g]["y"] = norm_last
            if s == NSUP - 1:
                st_["kv_free"] = att_pv_last
            if dbg and g == 0:
                tapd("d_y", y_sup[:], norm_last)
                tapd("d_yacc", y_acc[:], norm_last)

        def gen_f(seq, s):
            g = seq * NSUP + s
            y_sup = y_sup2[g % 2]
            norm_last = sup_tok[g]["y"]
            yfm_ev = None
            t_tp = None
            for qi in range(4):
                t_tp = transpose_group(
                    lambda c, qi_=qi: y_sup[:, qi_ * 1024 + c * 128:
                                            qi_ * 1024 + (c + 1) * 128],
                    8, [norm_last, st_["y_fm_free"]])
                yfm_ev = rec.op("vector", lambda v, qi_=qi: v.tensor_copy(
                    y_fm[:].rearrange("p (c x) -> p c x", x=512)
                    [:, :, qi_ * 128:(qi_ + 1) * 128],
                    pT[:].rearrange("p (c x) -> p c x", x=128)),
                    waits=[t_tp], inc="v")
                st_["pt_free"] = yfm_ev
                yield
            st_["ysup_free"][g % 2] = t_tp

            def f_lp(te):
                last = None
                for c in range(8):
                    last = mm(te, pC[0:32, :],
                              lap_bf[:, c * 32:(c + 1) * 32],
                              y_fm[:, c * 512:(c + 1) * 512], c == 0, c == 7)
                return last
            t_lpm = rec.op("tensor", f_lp,
                           waits=[yfm_ev, st_["pool_free"][2], cv_state["cv_tok"]],
                           inc="t")
            t_plp = rec.op("scalar", lambda s2: s2.activation(
                plp_bf[:, :], pC[0:32, :], ACT.Identity, scale=0.5),
                waits=[t_lpm], inc="s")
            st_["pool_free"][2] = t_plp
            yield

            last_proj_mm = None
            attn_ev = None
            for m in range(8):
                par = m % 2
                bi = m % 2
                bank = pool[bi]
                t_w = rec.dma("sync", "qw", wp_st[par][:],
                              wscr[:, OFF_WP + m * 1024:
                                   OFF_WP + (m + 1) * 1024],
                              waits=[st_["wp_free"][par], wtok_for(OFF_WP + m * 1024, 1024)])

                def f_pm(te, p=par, bk=bank, m_=m):
                    last = None
                    for c in range(8):
                        last = mm(te, bk[:],
                                  wp_st[p][:, c * 128:(c + 1) * 128],
                                  y_fm[:, c * 512:(c + 1) * 512],
                                  c == 0, False)
                    return mm(te, bk[:],
                              lbp_bf[0:32, m_ * 128:(m_ + 1) * 128],
                              plp_bf[0:32, :], False, True)
                t_mm2 = rec.op("tensor", f_pm,
                               waits=[t_w, yfm_ev, t_plp, cv_state["cv_tok"],
                                      st_["pool_free"][bi]], inc="t")
                st_["wp_free"][par] = t_mm2
                last_proj_mm = t_mm2
                t_ev = rec.op("scalar",
                              lambda s2, bk=bank, m_=m, p=par: s2.activation(
                                  po_bf[p][:], bk[:], ACT.Identity,
                                  bias=bp[:, m_:m_ + 1]),
                              waits=[t_mm2, t_bias, st_["po_free"][par]],
                              inc="s")
                st_["pool_free"][bi] = t_ev
                t_tp = transpose_group(
                    lambda c, p=par: po_bf[p][:, c * 128:(c + 1) * 128], 4,
                    [t_ev])
                st_["po_free"][par] = t_tp
                attn_ev = rec.op("vector", lambda v, m_=m: v.tensor_copy(
                    attn_tm[:].rearrange("p (q x) -> p q x", x=1024)
                    [:, :, m_ * 128:(m_ + 1) * 128],
                    pT[:, 0:512].rearrange("p (q x) -> p q x", x=128)),
                    waits=[t_tp, st_["attn_tm_free"]], inc="v")
                st_["pt_free"] = attn_ev
                yield
            st_["y_fm_free"] = last_proj_mm
            attn_ready = attn_ev
            if dbg and g == 0:
                tapd("d_yfm", y_fm[:], yfm_ev)
                tapd("d_attn", attn_tm[:], attn_ev)

            ln3_ev = None
            B2, B3 = 32, 64
            sq2_tok = None
            x_toks = []
            for qi in range(4):
                tch = s * 4 + qi
                a_sl = attn_tm[:, qi * 1024:(qi + 1) * 1024]
                t_x = rec.dma("gpsimd", "qx", x2t[qi][:],
                              x_view[seq, tch, :, :],
                              waits=[st_["x2_free"][qi]])
                x_toks.append(t_x)
                t_sum = rec.op("vector", lambda v, qi_=qi, a=a_sl: v.reduce_sum(
                    stat[:, B2 + qi_:B2 + qi_ + 1], a, axis=AX.X),
                    waits=[attn_ready], inc="v")
                sq2_tok = rec.op("vector", lambda v, qi_=qi, a=a_sl:
                                 v.scalar_tensor_tensor(
                                     out=scr_bf[:], in0=a, scalar=1.0, in1=a,
                                     op0=ALU.mult, op1=ALU.mult,
                                     accum_out=stat[:, B2 + 4 + qi_:B2 + 5 + qi_]),
                                 waits=[t_sum], inc="v")
            yield
            r2, mr2, tr2, tmr2 = emit_stats(sq2_tok, sq2_tok, B2)
            sq3_tok = None
            for qi in range(4):
                a_sl = attn_tm[:, qi * 1024:(qi + 1) * 1024]
                t_n2 = rec.op("scalar", lambda s2, qi_=qi, a=a_sl:
                              s2.activation(ot_bf[:], a, ACT.Identity,
                                            bias=mr2[:, qi_:qi_ + 1],
                                            scale=r2[:, qi_:qi_ + 1]),
                              waits=[tr2, tmr2, st_["ot_free"]], inc="s")
                t_add = rec.op("vector", lambda v, qi_=qi: v.tensor_add(
                    x2t[qi_][:], x2t[qi_][:], ot_bf[:]),
                    waits=[t_n2, x_toks[qi]], inc="v")
                st_["ot_free"] = t_add
                t_sum3 = rec.op("vector", lambda v, qi_=qi: v.reduce_sum(
                    stat[:, B3 + qi_:B3 + qi_ + 1], x2t[qi_][:], axis=AX.X),
                    waits=[t_add], inc="v")
                sq3_tok = rec.op("vector", lambda v, qi_=qi:
                                 v.scalar_tensor_tensor(
                                     out=scr_bf[:], in0=x2t[qi_][:],
                                     scalar=1.0, in1=x2t[qi_][:],
                                     op0=ALU.mult, op1=ALU.mult,
                                     accum_out=stat[:, B3 + 4 + qi_:B3 + 5 + qi_]),
                                 waits=[t_sum3], inc="v")
                yield
            r3, mr3, tr3, tmr3 = emit_stats(sq3_tok, sq3_tok, B3)
            t_n3 = None
            for qi in range(4):
                t_n3 = rec.op("scalar", lambda s2, qi_=qi:
                              s2.activation(ln3_tm[:], x2t[qi_][:],
                                            ACT.Identity,
                                            bias=mr3[:, qi_:qi_ + 1],
                                            scale=r3[:, qi_:qi_ + 1]),
                              waits=[tr3, tmr3, st_["ln3_tm_free"]],
                              inc="s")
                t_tp = transpose_group(
                    lambda c: ln3_tm[:, c * 128:(c + 1) * 128], 8,
                    [t_n3, st_["ln3_fm_free"]])
                st_["ln3_tm_free"] = t_tp
                ln3_ev = rec.op("vector", lambda v, qi_=qi: v.tensor_copy(
                    ln3_fm[:].rearrange("p (c x) -> p c x", x=512)
                    [:, :, qi_ * 128:(qi_ + 1) * 128],
                    pT[:].rearrange("p (c x) -> p c x", x=128)),
                    waits=[t_tp], inc="v")
                st_["pt_free"] = ln3_ev
                yield
            ln3_ready = ln3_ev
            st_["attn_tm_free"] = t_n3
            if dbg and g == 0:
                tapd("d_x2", x2t[0][:], st_["ot_free"])
                tapd("d_ln3", ln3_fm[:], ln3_ready)

            p_ready = None
            t_u1 = None
            for mh in range(2):
                par = mh % 2
                bank = pool[mh]
                t_wu1 = rec.dma("sync", "qw", wp_st[par][:],
                                wscr[:, OFF_U1 + mh * 1024:
                                     OFF_U1 + (mh + 1) * 1024],
                                waits=[st_["wp_free"][par], wtok_for(OFF_U1 + mh * 1024, 1024)])

                def f_u1(te, p=par, bk=bank):
                    last = None
                    for c in range(8):
                        last = mm(te, bk[:],
                                  wp_st[p][:, c * 128:(c + 1) * 128],
                                  ln3_fm[:, c * 512:(c + 1) * 512],
                                  c == 0, c == 7)
                    return last
                t_u1 = rec.op("tensor", f_u1,
                              waits=[t_wu1, ln3_ready, st_["pool_free"][mh]],
                              inc="t")
                st_["wp_free"][par] = t_u1
                p_ready = rec.op("vector",
                                 lambda v, mh_=mh, bk=bank: v.tensor_copy(
                                     P_bf[:, mh_ * 512:(mh_ + 1) * 512],
                                     bk[:]),
                                 waits=[t_u1], inc="v")
                st_["pool_free"][mh] = p_ready
                yield

            t_u2 = None
            t_wv4 = [None, None]
            t_wu4 = [None, None]
            for fc in range(NFC):
                par = fc % 2
                grp = fc // 2
                gp = grp % 2
                co = (fc % 2) * 256
                if fc % 2 == 0:
                    t_wv4[gp] = rec.dma(
                        "sync", "qw", vu_st[gp][:],
                        wscr[:, OFF_V1 + grp * 512:
                             OFF_V1 + (grp + 1) * 512],
                        waits=[st_["vu_free"][gp],
                               wtok_for(OFF_V1 + grp * 512, 512)])
                t_wv = t_wv4[gp]
                t_wu = rec.dma("sync", "qw", u2_st[par][:],
                               wscr[:, OFF_U2 + fc * 256:
                                    OFF_U2 + (fc + 1) * 256],
                               waits=[st_["u2_free"][par],
                                      wtok_for(OFF_U2 + fc * 256, 256)])

                def f_v1(te, p=gp, c0=co):
                    mm(te, pC[:], vu_st[p][:, c0:c0 + 128], P_bf[:, 0:512],
                       True, False)
                    return mm(te, pC[:], vu_st[p][:, c0 + 128:c0 + 256],
                              P_bf[:, 512:1024], False, True)
                t_v1 = rec.op("tensor", f_v1,
                              waits=[t_wv, p_ready, st_["pool_free"][2]],
                              inc="t")
                if fc % 2 == 1:
                    st_["vu_free"][gp] = t_v1
                t_g = rec.op("scalar", lambda s2, p=par, fc_=fc:
                             s2.activation(h_bf[p][:], pC[:],
                                           ACT.Gelu_apprx_tanh,
                                           bias=b1c[:, fc_:fc_ + 1]),
                             waits=[t_v1, st_["h_free"][par], t_bias],
                             inc="s")
                st_["pool_free"][2] = t_g

                def f_u2(te, p=par, fc_=fc):
                    mm(te, pA[:], u2_st[p][:, 0:128], h_bf[fc_ % 2][:],
                       fc_ == 0, fc_ == NFC - 1)
                    return mm(te, pB[:], u2_st[p][:, 128:256],
                              h_bf[fc_ % 2][:],
                              fc_ == 0, fc_ == NFC - 1)
                w_u2 = [t_wu, t_g]
                if fc == 0:
                    w_u2 += [st_["pool_free"][0], st_["pool_free"][1]]
                t_u2 = rec.op("tensor", f_u2, waits=w_u2, inc="t")
                st_["u2_free"][par] = t_u2
                st_["h_free"][par] = t_u2

            def f_q2(v):
                v.tensor_copy(q2_bf[:, 0:512], pA[:])
                return v.tensor_copy(q2_bf[:, 512:1024], pB[:])
            t_q2 = rec.op("vector", f_q2,
                          waits=[t_u2, st_["q2_free"]], inc="v")
            st_["pool_free"][0] = t_q2
            st_["pool_free"][1] = t_q2
            yield
            if dbg and g == 0:
                tapd("d_P", P_bf[:], p_ready)
                tapd("d_q2", q2_bf[:], t_q2)

            t_v2 = None
            for qi in range(4):
                tch = s * 4 + qi
                b4_ = 96 + (qi % 2) * 16

                def f_v2(te, qi_=qi):
                    last = None
                    for seg in range(2):
                        bk = pool[seg]
                        for ch in range(2):
                            last = mm(te, bk[:],
                                      q2_bf[:, ch * 512 + qi_ * 128:
                                            ch * 512 + (qi_ + 1) * 128],
                                      v2_bf[:, ch * 1024 + seg * 512:
                                            ch * 1024 + (seg + 1) * 512],
                                      ch == 0, ch == 1)
                    return last
                t_v2 = rec.op("tensor", f_v2,
                              waits=[t_q2, st_["pool_free"][0],
                                     st_["pool_free"][1], cv_state["cv_tok"]], inc="t")

                def f_b2(v, b=b4_):
                    v.scalar_tensor_tensor(
                        out=o1[:, 0:512], in0=pA[:], scalar=1.0,
                        in1=b2r[:, 0:512], op0=ALU.mult, op1=ALU.add,
                        accum_out=stat[:, b + 6:b + 7])
                    return v.scalar_tensor_tensor(
                        out=o1[:, 512:1024], in0=pB[:], scalar=1.0,
                        in1=b2r[:, 512:1024], op0=ALU.mult, op1=ALU.add,
                        accum_out=stat[:, b + 7:b + 8])
                t_b2o = rec.op("vector", f_b2,
                               waits=[t_v2, t_bias2, st_["o1_free"]],
                               inc="v")
                st_["pool_free"][0] = t_b2o
                st_["pool_free"][1] = t_b2o
                t_sum4 = rec.op("vector", lambda v, b=b4_: v.tensor_add(
                    stat[:, b:b + 1], stat[:, b + 6:b + 7],
                    stat[:, b + 7:b + 8]), waits=[t_b2o], inc="v")
                t_sq4 = rec.op("vector", lambda v, b=b4_:
                               v.scalar_tensor_tensor(
                                   out=scr_bf[:], in0=o1[:], scalar=1.0,
                                   in1=o1[:], op0=ALU.mult, op1=ALU.mult,
                                   accum_out=stat[:, b + 1:b + 2]),
                               waits=[t_b2o], inc="v")
                r4v, mr4v, tr4, tmr4 = emit_stats(t_sum4, t_sq4, b4_, nb=1)
                t_aff = rec.op("vector", lambda v, r_=r4v, m_=mr4v: v.tensor_scalar(
                    out=o1[:], in0=o1[:], scalar1=r_, scalar2=m_,
                    op0=ALU.mult, op1=ALU.add),
                    waits=[tr4, tmr4, t_sq4], inc="v")
                t_fin = rec.op("vector", lambda v, qi_=qi: v.tensor_add(
                    x2t[qi_][:], x2t[qi_][:], o1[:]), waits=[t_aff], inc="v")
                t_st = rec.dma("gpsimd", "qx", out_view[seq, tch, :, :],
                               x2t[qi][:], waits=[t_fin])
                st_["x2_free"][qi] = t_st
                yield
            st_["ln3_fm_free"] = t_u1

        def mk(kind, g):
            sq, sp = g // NSUP, g % NSUP
            if kind == "conv":
                return gen_conv()
            if kind == "p2":
                return gen_p2(sq, sp)
            if kind == "att":
                return gen_att(sq, sp)
            return gen_f(sq, sp)

        slots = [
            [("conv", 0), ("p2", 0)],
            [("att", 0), ("p2", 1)],
            [("att", 1), ("p2", 2), ("f", 0)],
            [("att", 2), ("p2", 3), ("f", 1)],
            [("att", 3), ("f", 2)],
            [("p2", 4), ("f", 3)],
            [("att", 4), ("p2", 5)],
            [("att", 5), ("p2", 6), ("f", 4)],
            [("att", 6), ("p2", 7), ("f", 5)],
            [("att", 7), ("f", 6)],
            [("f", 7)],
        ]
        if dbg:
            slots = ([[("p2", g)], [("att", g)], [("f", g)]]
                     for g in range(SEQS * NSUP))
            slots = [[("conv", 0)]] + [x for g in slots for x in g]
        for slot in slots:
            alive = [(k, g, mk(k, g)) for (k, g) in slot]
            while alive:
                for item in list(alive):
                    k, g, gn = item
                    try:
                        next(gn)
                    except StopIteration:
                        alive.remove(item)
                        if k == "att":
                            done_att[g] = True

        rec.op("gpsimd", lambda g: g.nop(), waits=[("qx", rec.cnt["qx"])])

        sems = {}
        for name in ("qw", "qx", "qs", "g", "v", "s", "t"):
            sems[name] = es.enter_context(nc.semaphore(name))

        block = es.enter_context(nc.Block())

        def replay(engname):
            def runner(eng):
                last_wait = {}
                for waits, inc_sem, amt, f in rec.q[engname]:
                    for (sn, val) in waits:
                        if last_wait.get(sn, -1) >= val:
                            continue
                        eng.wait_ge(sems[sn], val)
                        last_wait[sn] = val
                    ins = f(eng)
                    if inc_sem is not None and ins is not None:
                        ins.then_inc(sems[inc_sem], amt)
            return runner

        block.sync(replay("sync"))
        block.gpsimd(replay("gpsimd"))
        block.vector(replay("vector"))
        block.scalar(replay("scalar"))
        block.tensor(replay("tensor"))

    return nc


def make_images(inputs):
    f = np.float32
    W = np.asarray(inputs["L0_attn_w_attn"], f)
    b_attn = np.asarray(inputs["L0_attn_b_attn"], f)
    la = np.asarray(inputs["L0_attn_la_attn"], f)
    lb = np.asarray(inputs["L0_attn_lb_attn"], f)
    Wp = np.asarray(inputs["L0_attn_w_proj"], f)
    b_proj = np.asarray(inputs["L0_attn_b_proj"], f)
    lap = np.asarray(inputs["L0_attn_la_proj"], f)
    lbp = np.asarray(inputs["L0_attn_lb_proj"], f)
    U1 = np.asarray(inputs["L0_ffn_U1"], f)
    V1 = np.asarray(inputs["L0_ffn_V1"], f)
    b1 = np.asarray(inputs["L0_ffn_b1"], f)
    U2 = np.asarray(inputs["L0_ffn_U2"], f)
    V2 = np.asarray(inputs["L0_ffn_V2"], f)
    b2 = np.asarray(inputs["L0_ffn_b2"], f)

    qkv = W.T.reshape(8, 128, 3072).transpose(1, 0, 2)
    m_slices = [np.ascontiguousarray(qkv[:, :, m * 128:(m + 1) * 128]
                                     .reshape(128, 1024)) for m in range(24)]
    la_img = np.ascontiguousarray(la.T.reshape(8, 128, 32).transpose(1, 0, 2)
                                  .reshape(128, 256))
    wpT = Wp.T.reshape(8, 128, 1024).transpose(1, 0, 2)
    wp_slices = [np.ascontiguousarray(wpT[:, :, m * 128:(m + 1) * 128]
                                      .reshape(128, 1024)) for m in range(8)]
    v1r = V1.reshape(2, 128, 4096).transpose(1, 0, 2)
    v1_slices = [np.ascontiguousarray(v1r[:, :, fc * 128:(fc + 1) * 128]
                                      .reshape(128, 256)) for fc in range(32)]
    u2r = U2.reshape(32, 128, 256).transpose(1, 0, 2)
    u2_slices = [np.ascontiguousarray(u2r[:, fc, :].reshape(128, 256))
                 for fc in range(32)]
    u1r = U1.reshape(8, 128, 256).transpose(1, 0, 2)
    u1_slices = [np.ascontiguousarray(u1r[:, :, mh * 128:(mh + 1) * 128]
                                      .reshape(128, 1024)) for mh in range(2)]
    wbig = np.ascontiguousarray(
        np.concatenate(m_slices + [la_img] + wp_slices + v1_slices + u2_slices
                       + u1_slices, axis=1))
    assert wbig.shape == (128, WCOLS), wbig.shape

    v2_img = V2.reshape(2, 128, 1024).transpose(1, 0, 2).reshape(128, 2048)
    lap_img = lap.T.reshape(8, 128, 32).transpose(1, 0, 2).reshape(128, 256)
    wsmall = np.ascontiguousarray(
        np.concatenate([v2_img, lap_img], axis=1))

    wlb = np.zeros((32, 4096), f)
    wlb[:, 0:3072] = lb.T
    wlb[:, 3072:4096] = lbp.T

    return dict(
        wbig=wbig, wsmall=wsmall, wlb=wlb,
        bqk=np.ascontiguousarray(b_attn.reshape(24, 128).T),
        b1c=np.ascontiguousarray(b1.reshape(32, 128).T),
        bp=np.ascontiguousarray(b_proj.reshape(8, 128).T),
        b2r=np.ascontiguousarray(np.broadcast_to(b2[None, :], (128, 1024))),
    )


def kernel(**inputs):
    imgs = make_images(inputs)
    x = np.asarray(inputs["L0_attn_x"], np.float32)
    nc = build_nc()
    in_maps = []
    for c in range(NCORE):
        m = dict(imgs)
        m["x"] = np.ascontiguousarray(x[SEQS * c:SEQS * (c + 1)])
        in_maps.append(m)
    res = run_bass_kernel_spmd(nc, in_maps, core_ids=list(range(NCORE)))
    out = np.empty_like(x)
    for c in range(NCORE):
        out[SEQS * c:SEQS * (c + 1)] = res.results[c]["out"]
    return out
